# version 23
# baseline (speedup 1.0000x reference)
import os
os.environ.setdefault("TRN_ALLOW_ICI_ALL_TO_ALL", "1")

import numpy as np
import concourse.bass as bass
import concourse.mybir as mybir
from concourse.bass_utils import run_bass_kernel_spmd

F32 = mybir.dt.float32
BF16 = mybir.dt.bfloat16
AF = mybir.ActivationFunctionType
ALU = mybir.AluOpType
AX = mybir.AxisListType

NCORES = 8
B = 2
T = 8192
DM = 1024
DI = 2048
NST = 16
DTR = 32
NT = 2048
HALF = 1024
MYCH = 256
TC = 512
NCH = T // TC
TT = 512
TTL = 512
RG = [list(range(NCORES))]
KB = 1024

ENGS = ("sync", "gpsimd", "vector", "scalar", "tensor")
DEBUG = False

BV = {"dvm": 0, "b1g0": 1, "b1g1": 33, "b2g0": 65, "b2g1": 73,
      "bix": 81, "biz": 97, "bo": 113, "bdt": 121, "convb": 123,
      "hb2g0": 125, "hb2g1": 133, "eps": 141, "one": 142}
NB_DMA = 124


class Sem:
    __slots__ = ("name", "count", "h")

    def __init__(self, name):
        self.name, self.count, self.h = name, 0, None


class Prog:
    def __init__(self):
        self.streams = {e: [] for e in ENGS}
        self.sems = {}
        self.marks = {e: {} for e in ENGS}

    def sem(self, name):
        if name not in self.sems:
            self.sems[name] = Sem(name)
        return self.sems[name]

    def wait(self, eng, sem, val=None):
        v = sem.count if val is None else val
        if v <= 0 or self.marks[eng].get(sem.name, -1) >= v:
            return
        self.marks[eng][sem.name] = v
        self.streams[eng].append(lambda e, s=sem, v=v: e.wait_ge(s.h, v))

    def op(self, eng, fn, incs=()):
        for s, n in incs:
            s.count += n

        def clo(e, fn=fn, incs=tuple(incs)):
            inst = fn(e)
            for s, n in incs:
                inst.then_inc(s.h, n)
            return inst
        self.streams[eng].append(clo)


def build_nc():
    nc = bass.Bass("TRN2", target_bir_lowering=False)
    p = Prog()
    W = p.wait
    O = p.op

    def din(name, shape):
        return nc.dram_tensor(name, list(shape), F32, kind="ExternalInput").ap()

    x_d = din("xT_blk", (DM, NT))
    WiT_d = din("WiT", (DM, 2 * DI))
    WoT_d = din("WoT", (DI, DM))
    W1T_d = [din(f"W1Tg{i}", (DM, 4096)) for i in range(2)]
    W2T_d = [din(f"W2Tg{i}", (4096, DM)) for i in range(2)]
    WdtT_d = din("WdtT_my", (DTR, MYCH))
    biascat_d = din("biascat", (128, NB_DMA))
    convw_d = din("convw_my", (MYCH, 4))
    Alog_d = din("Alog_my", (MYCH, NST))
    Bm_d = din("Bm_my", (MYCH, NST))
    Cm_d = din("Cm_my", (MYCH, NST))
    Dv_d = din("Dv", (DI,))
    out_d = nc.dram_tensor("out", [8, 128, NT], F32, kind="ExternalOutput").ap()
    if DEBUG:
        dbg_xln = nc.dram_tensor("dbg_xln", [8, 128, NT], BF16,
                                 kind="ExternalOutput").ap()
        dbg_xres = nc.dram_tensor("dbg_xres", [8, 128, NT], F32,
                                  kind="ExternalOutput").ap()
        s_dbg = p.sem("s_dbg")

    send1 = [nc.dram_tensor(f"send1{b}", [NCORES, MYCH, HALF], F32,
                            kind="Internal").ap() for b in range(2)]
    recv1 = [nc.dram_tensor(f"recv1{b}", [NCORES, MYCH, HALF], F32,
                            kind="Internal").ap() for b in range(2)]
    ag_s = [nc.dram_tensor(f"ag_s{b}", [DTR, HALF], F32,
                           kind="Internal").ap() for b in range(2)]
    ag_r = [nc.dram_tensor(f"ag_r{b}", [NCORES, DTR, HALF], F32,
                           kind="Internal").ap() for b in range(2)]
    ysend = [nc.dram_tensor(f"ysend{b}", [NCORES, MYCH, HALF], F32,
                            kind="Internal").ap() for b in range(2)]
    yrecv = [nc.dram_tensor(f"yrecv{b}", [NCORES, MYCH, HALF], F32,
                            kind="Internal").ap() for b in range(2)]
    wsp_w1 = nc.dram_tensor("wsp_w1", [2, 2, 8, 128, 2048], BF16, kind="Internal").ap()
    wsp_w2 = nc.dram_tensor("wsp_w2", [2, 2, 16, 128, 1024], BF16, kind="Internal").ap()
    wsp_wiz = nc.dram_tensor("wsp_wiz", [8, 128, 2048], BF16, kind="Internal").ap()
    wsp_wo = nc.dram_tensor("wsp_wo", [16, 128, 1024], BF16, kind="Internal").ap()

    def S(name, shape, dt=F32):
        return nc.alloc_sbuf_tensor(name, list(shape), dt).ap()

    x_res = [S(f"xres{i}", (128, NT)) for i in range(8)]
    x_ln = [S(f"xln{i}", (128, NT), BF16) for i in range(8)]
    biasv = S("biasv", (128, 144))
    ident_f = S("ident_f", (128, 128))
    ident_b = S("ident_b", (128, 128), BF16)
    ones_colf = S("ones_colf", (128, 1))
    ones_colb = S("ones_colb", (128, 1), BF16)
    ones_rowf = S("ones_rowf", (1, 128))
    aneg = [S(f"aneg{d}", (128, NST)) for d in range(2)]
    gvec = [S(f"gvec{d}", (128, NST)) for d in range(2)]
    convw_s = [S(f"convw{d}", (128, 4)) for d in range(2)]
    cst16 = [S(f"cst16_{i}", (128, NST)) for i in range(2)]
    dv16 = S("dv16", (128, NST))
    wdtT = S("wdtT", (DTR, MYCH), BF16)
    wdtstg = S("wdtstg", (DTR, MYCH))

    POOL_KB = 96
    pool = nc.alloc_sbuf_tensor("pool", [128, POOL_KB * KB], mybir.dt.uint8)
    pool_base = nc.lookup_mloc(pool).addr

    def V(name, shape, dt, off_bytes):
        per_part = int(np.prod(shape[1:])) * mybir.dt.size(dt)
        assert off_bytes + per_part <= POOL_KB * KB, (name, off_bytes, per_part)
        h = nc.virtual_sbuf_tensor(name, list(shape), dt)
        nc._tensor(h.name, h.shape, h.dtype, type="SB")
        mloc = nc.lookup_mloc(h)
        mloc.allocated = True
        mloc.addr = pool_base + off_bytes
        return h.ap()

    FT = 512
    wta = [V(f"wta{i}", (128, 2048), BF16, (0 + 4 * i) * KB) for i in range(8)]
    wtb = [V(f"wtb{i}", (128, 1024), BF16, (32 + 2 * i) * KB) for i in range(16)]
    kt16 = [V(f"kt16_{i}", (128, FT), BF16, (64 + i) * KB) for i in range(16)]
    wsl = [V(f"wsl{i}", (128, 1024), F32, (80 + 4 * i) * KB) for i in range(4)]
    wslp = [V(f"wslp{i}", (128, 2048), F32, (80 + 8 * i) * KB) for i in range(2)]
    sqb = [V(f"sqb{i}", (128, TTL), BF16, (64 + i) * KB) for i in range(2)]
    xcb = [V(f"xcb{i}", (128, TTL), BF16, (66 + i) * KB) for i in range(2)]
    lntmp = V("lntmp", (128, TTL), F32, 68 * KB)
    lnA = V("lnA", (1, TTL), F32, 70 * KB)
    lnB = V("lnB", (1, TTL), F32, 72 * KB)
    lnC = V("lnC", (1, TTL), F32, 74 * KB)
    NGS = 4
    gbuf = V("gbuf", (128, NGS * TC), F32, 0 * KB)
    hbuf = [V(f"hbuf{i}", (128, NST * TC), BF16, (8 + 16 * i) * KB) for i in range(2)]
    mx = 40 * KB

    def MV(name, shape, dt):
        nonlocal mx
        per_part = int(np.prod(shape[1:])) * mybir.dt.size(dt)
        ap = V(name, shape, dt, mx)
        mx += (per_part + 31) // 32 * 32
        return ap

    ysb = [MV(f"ysb{i}", (128, TC), F32) for i in range(2)]
    cxb = [MV(f"cx{i}", (128, TC + 3), F32) for i in range(4)]
    cxbb = [MV(f"cxbb{i}", (128, TC + 3), BF16) for i in range(2)]
    dtb = [MV(f"dt{i}", (128, TC), F32) for i in range(2)]
    sigb = [MV(f"sig{i}", (128, TC), F32) for i in range(2)]
    vbuf = [MV(f"v{i}", (128, TC), F32) for i in range(2)]
    ebuf = MV("ebuf", (128, TC), F32)
    x32f = [MV(f"x32f{i}", (DTR, TC), F32) for i in range(4)]
    x32b = [MV(f"x32b{i}", (DTR, TC), BF16) for i in range(2)]
    gdiag = [MV(f"gdt{i}", (128, 128), BF16) for i in range(32)]
    cdiag = [MV(f"cdt{i}", (128, 128), BF16) for i in range(8)]
    assert mx <= 96 * KB, mx
    yld = [V(f"yld{i}", (128, FT), F32, (80 + 2 * i) * KB) for i in range(4)]
    szb = V("szb", (128, FT), F32, 88 * KB)

    PSA = nc.alloc_psum_tensor("psa", [128, 512], F32).ap()
    PSB = nc.alloc_psum_tensor("psb", [128, 512], F32).ap()
    PSW = nc.alloc_psum_tensor("psw", [128, 2048], F32).ap()
    PSYx = nc.alloc_psum_tensor("psy", [128, 2 * TC], F32).ap()
    PSY = [PSYx[:, 0:TC], PSYx[:, TC:2 * TC]]

    s_pe = p.sem("s_pe")
    s_act = p.sem("s_act")
    s_dve = p.sem("s_dve")
    s_gps = p.sem("s_gps")
    s_x = p.sem("s_x")
    s_w = p.sem("s_w")
    s_wr = p.sem("s_wr")
    s_sp = p.sem("s_sp")
    s_wl = p.sem("s_wl")
    s_cc = p.sem("s_cc")
    s_cx = [p.sem("s_cx0"), p.sem("s_cx1")]
    s_yst = p.sem("s_yst")
    s_yld = [p.sem(f"s_yld{i}") for i in range(4)]
    s_st4 = [p.sem(f"s_st4_{i}") for i in range(4)]
    s_out = p.sem("s_out")

    def bcol(tag, i=0):
        return biasv[:, BV[tag] + i:BV[tag] + i + 1]

    O("gpsimd", lambda e: e.memset(ident_f[:], 0.0), [(s_gps, 1)])
    W("gpsimd", s_gps)
    O("gpsimd", lambda e: e.affine_select(
        out=ident_f[:], in_=ident_f[:], compare_op=ALU.not_equal, fill=1.0,
        base=0, pattern=[[-1, 128]], channel_multiplier=1), [(s_gps, 1)])
    W("gpsimd", s_gps)
    O("gpsimd", lambda e: e.tensor_copy(out=ident_b[:], in_=ident_f[:]), [(s_gps, 1)])
    O("gpsimd", lambda e: e.memset(ones_colf[:], 1.0), [(s_gps, 1)])
    O("gpsimd", lambda e: e.memset(ones_colb[:], 1.0), [(s_gps, 1)])
    O("gpsimd", lambda e: e.memset(ones_rowf[:], 1.0), [(s_gps, 1)])
    O("gpsimd", lambda e: e.memset(biasv[:, BV["eps"]:BV["eps"] + 1], 1e-5),
      [(s_gps, 1)])
    O("gpsimd", lambda e: e.memset(biasv[:, BV["one"]:BV["one"] + 1], 1.0),
      [(s_gps, 1)])
    GPS_CONST = s_gps.count

    for d in range(8):
        O("sync", lambda e, d=d: e.dma_start(
            out=x_res[d][:], in_=x_d[d * 128:(d + 1) * 128, :]), [(s_x, 16)])

    def ld(dst, src, sem=s_w):
        O("sync", lambda e, dst=dst, src=src: e.dma_start(out=dst, in_=src),
          [(sem, 16)])

    ld(biasv[:, 1:1 + NB_DMA], biascat_d[:, :])
    ld(cst16[0][:], Bm_d[0:128, :])
    ld(cst16[1][:], Bm_d[128:256, :])
    ld(aneg[0][:], Cm_d[0:128, :])
    ld(aneg[1][:], Cm_d[128:256, :])
    ld(dv16[:], Dv_d.rearrange("(a b) -> a b", a=128))
    ld(convw_s[0][:], convw_d[0:128, :])
    ld(convw_s[1][:], convw_d[128:256, :])
    ld(wdtstg[:], WdtT_d[:, :])
    W("vector", s_w)
    for d in range(2):
        O("vector", lambda e, d=d: e.tensor_mul(
            out=gvec[d][:], in0=cst16[d][:], in1=aneg[d][:]), [(s_dve, 1)])
    GV_DONE = s_dve.count
    W("sync", s_dve, GV_DONE)
    ld(cst16[0][:], Alog_d[0:128, :])
    ld(cst16[1][:], Alog_d[128:256, :])
    W("scalar", s_w)
    for d in range(2):
        O("scalar", lambda e, d=d: e.activation(
            out=cst16[d][:], in_=cst16[d][:], func=AF.Exp, scale=1.0),
            [(s_act, 1)])
    O("scalar", lambda e: e.activation(
        out=wdtT[:], in_=wdtstg[:], func=AF.Copy, scale=1.0), [(s_act, 1)])
    W("vector", s_act)
    for d in range(2):
        O("vector", lambda e, d=d: e.tensor_scalar_mul(
            out=aneg[d][:], in0=cst16[d][:], scalar1=-1.0), [(s_dve, 1)])
    ANEG_DONE = s_dve.count
    for g in range(2):
        O("vector", lambda e, g=g: e.tensor_scalar_mul(
            out=biasv[:, BV[f"hb2g{g}"]:BV[f"hb2g{g}"] + 8],
            in0=biasv[:, BV[f"b2g{g}"]:BV[f"b2g{g}"] + 8], scalar1=0.5),
            [(s_dve, 1)])

    O("vector", lambda e: e.reduce_sum(
        out=dv16[:, 0:1], in_=dv16[:], axis=AX.X), [(s_dve, 1)])
    W("tensor", s_dve)
    W("tensor", s_gps, GPS_CONST)
    O("tensor", lambda e: e.transpose(
        out=PSA[0:1, 0:128], in_=dv16[:, 0:1],
        identity=ident_f[:]), [(s_pe, 1)])
    W("vector", s_pe)
    O("vector", lambda e: e.reduce_sum(
        out=wdtstg[0:1, 0:1], in_=PSA[0:1, 0:128], axis=AX.X), [(s_dve, 1)])
    W("tensor", s_dve)
    O("tensor", lambda e: e.matmul(
        out=PSB[0:128, 0:1], lhsT=ones_rowf[:], rhs=wdtstg[0:1, 0:1],
        start=True, stop=True), [(s_pe, 1)])
    W("scalar", s_pe)
    O("scalar", lambda e: e.activation(
        out=biasv[:, 0:1], in_=PSB[0:128, 0:1], func=AF.Copy, scale=1.0 / DI),
        [(s_act, 1)])

    ln_state = {"first": True}

    def emit_ln():
        grp_done = []
        for g in range(NT // TTL):
            c0 = g * TTL
            sq_pe = {}
            for d in range(8):
                pp = d % 2
                if ln_state["first"]:
                    W("scalar", s_x)
                    W("vector", s_x)
                W("scalar", s_dve)
                W("scalar", s_pe, sq_pe.get(d - 2))
                O("scalar", lambda e, d=d, pp=pp, c0=c0: e.activation(
                    out=sqb[pp][:], in_=x_res[d][:, c0:c0 + TTL],
                    func=AF.Square, scale=1.0), [(s_act, 1)])
                W("vector", s_pe, sq_pe.get(d - 2))
                O("vector", lambda e, d=d, pp=pp, c0=c0: e.tensor_copy(
                    out=xcb[pp][:], in_=x_res[d][:, c0:c0 + TTL]), [(s_dve, 1)])
                if ln_state["first"]:
                    W("tensor", s_gps, GPS_CONST)
                W("tensor", s_dve)
                W("tensor", s_act)
                O("tensor", lambda e, d=d, pp=pp: e.matmul(
                    out=PSA[0:1, 0:TTL], lhsT=ones_colb[:], rhs=xcb[pp][:],
                    start=(d == 0), stop=(d == 7)), [])
                O("tensor", lambda e, d=d, pp=pp: e.matmul(
                    out=PSB[0:1, 0:TTL], lhsT=ones_colb[:], rhs=sqb[pp][:],
                    start=(d == 0), stop=(d == 7)), [(s_pe, 1)])
                sq_pe[d] = s_pe.count
            ln_state["first"] = False
            W("vector", s_pe)
            O("vector", lambda e: e.tensor_scalar_mul(
                out=lnA[0:1, :], in0=PSA[0:1, 0:TTL], scalar1=1.0 / DM),
                [(s_dve, 1)])
            W("vector", s_dve)
            O("vector", lambda e: e.tensor_scalar_mul(
                out=lnB[0:1, :], in0=PSB[0:1, 0:TTL], scalar1=1.0 / DM),
                [(s_dve, 1)])
            W("vector", s_dve)
            O("vector", lambda e: e.tensor_mul(
                out=lnC[0:1, :], in0=lnA[0:1, :], in1=lnA[0:1, :]),
                [(s_dve, 1)])
            W("vector", s_dve)
            O("vector", lambda e: e.tensor_sub(
                out=lnB[0:1, :], in0=lnB[0:1, :], in1=lnC[0:1, :]),
                [(s_dve, 1)])
            W("scalar", s_dve)
            O("scalar", lambda e: e.activation(
                out=lnC[0:1, :], in_=lnB[0:1, :], func=AF.Sqrt,
                scale=1.0, bias=biasv[0:1, BV["eps"]:BV["eps"] + 1]),
                [(s_act, 1)])
            W("vector", s_act)
            O("vector", lambda e: e.reciprocal(
                out=lnB[0:1, :], in_=lnC[0:1, :]), [(s_dve, 1)])
            W("tensor", s_dve)
            O("tensor", lambda e: e.matmul(
                out=PSA[:, 0:TTL], lhsT=ones_rowf[:], rhs=lnA[0:1, :],
                start=True, stop=True), [(s_pe, 1)])
            O("tensor", lambda e: e.matmul(
                out=PSB[:, 0:TTL], lhsT=ones_rowf[:], rhs=lnB[0:1, :],
                start=True, stop=True), [(s_pe, 1)])
            W("vector", s_pe)
            for d in range(8):
                O("vector", lambda e, d=d, c0=c0: e.tensor_sub(
                    out=lntmp[:], in0=x_res[d][:, c0:c0 + TTL],
                    in1=PSA[:, 0:TTL]), [(s_dve, 1)])
                W("vector", s_dve)
                O("vector", lambda e, d=d, c0=c0: e.tensor_mul(
                    out=x_ln[d][:, c0:c0 + TTL], in0=lntmp[:],
                    in1=PSB[:, 0:TTL]), [(s_dve, 1)])
                W("vector", s_dve)
            grp_done.append(s_dve.count)
        return grp_done

    wl_state = {"slot_free": {}}

    def emit_wload(src_rows, ntiles, wide, dst, spill=None, pe_gate=None):
        eng = "vector"
        esem = s_dve
        nslot = 2 if wide else 4
        sp0 = s_sp.count
        pe0 = s_pe.count if pe_gate is None else pe_gate
        dma_done = {}
        cast_done = {}
        for k in range(ntiles):
            si = k % nslot
            slot = wslp[si] if wide else wsl[si]
            for sk in ([2 * si, 2 * si + 1] if wide else [si]):
                fr = wl_state["slot_free"].get(sk)
                if fr is not None:
                    W("sync", fr[0], fr[1])
            if k - nslot in cast_done:
                W("sync", esem, cast_done[k - nslot])
            O("sync", lambda e, k=k, slot=slot: e.dma_start(
                out=slot[:], in_=src_rows(k)), [(s_wr, 16)])
            dma_done[k] = s_wr.count
            W(eng, s_wr, dma_done[k])
            W(eng, s_pe, pe0)
            W(eng, s_sp, sp0)
            O(eng, lambda e, k=k, slot=slot: e.tensor_copy(
                out=dst[k][:], in_=slot[:]), [(esem, 1)])
            cast_done[k] = esem.count
            for sk in ([2 * si, 2 * si + 1] if wide else [si]):
                wl_state["slot_free"][sk] = (esem, cast_done[k])
            if spill is not None:
                W("sync", esem, cast_done[k])
                O("sync", lambda e, k=k: e.dma_start(
                    out=spill(k), in_=dst[k][:, :]), [(s_sp, 16)])
        return (esem, cast_done[ntiles - 1])

    def emit_ffn(which):
        for g in range(2):
            for d in range(8):
                W("vector", s_pe)
                W("vector", s_act)
                W("vector", s_dve)
                O("vector", lambda e, g=g, d=d: e.tensor_scalar_add(
                    out=x_res[d][:, g * HALF:(g + 1) * HALF],
                    in0=x_res[d][:, g * HALF:(g + 1) * HALF],
                    scalar1=bcol(f"hb2g{g}", d)), [(s_dve, 1)])
        phases = [(0, 0), (0, 1), (1, 0), (1, 1)]
        base_dve = s_dve.count

        def load_a(p, pe_gate=None):
            g, half = phases[p]
            if which == 0:
                return emit_wload(
                    lambda k, g=g, half=half: W1T_d[g][
                        128 * k:128 * (k + 1),
                        2048 * half:2048 * (half + 1)],
                    8, True, wta, pe_gate=pe_gate,
                    spill=lambda k, g=g, half=half: wsp_w1[g, half, k])
            W("sync", s_pe, pe_gate)
            for kt in range(8):
                O("sync", lambda e, kt=kt, g=g, half=half: e.dma_start(
                    out=wta[kt][:], in_=wsp_w1[g, half, kt]), [(s_wl, 16)])
            return (s_wl, s_wl.count)

        def load_b(p, pe_gate=None):
            g, half = phases[p]
            if which == 0:
                return emit_wload(
                    lambda k, g=g, half=half: W2T_d[g][
                        2048 * half + 128 * k:2048 * half + 128 * (k + 1), :],
                    16, False, wtb, pe_gate=pe_gate,
                    spill=lambda k, g=g, half=half: wsp_w2[g, half, k])
            W("sync", s_pe, pe_gate)
            for kt in range(16):
                O("sync", lambda e, kt=kt, g=g, half=half: e.dma_start(
                    out=wtb[kt][:], in_=wsp_w2[g, half, kt]), [(s_wl, 16)])
            return (s_wl, s_wl.count)

        pg0 = 0 if which == 0 else None
        rdy = {(0, 'a'): load_a(0, pe_gate=pg0),
               (0, 'b'): load_b(0, pe_gate=pg0)}
        last_stt = base_dve
        for p, (g, half) in enumerate(phases):
            ra = rdy[(p, 'a')]
            rb = rdy[(p, 'b')]
            for tt in range(2):
                c0 = g * HALF + tt * FT
                W("tensor", s_dve, base_dve)
                W("tensor", ra[0], ra[1])
                for m1 in range(16):
                    pst = (PSA if m1 % 2 == 0 else PSB)[:, 0:FT]
                    W("tensor", s_act)
                    for k in range(8):
                        O("tensor", lambda e, m1=m1, k=k, c0=c0, pst=pst: e.matmul(
                            out=pst, lhsT=wta[k][:, m1 * 128:(m1 + 1) * 128],
                            rhs=x_ln[k][:, c0:c0 + FT],
                            start=(k == 0), stop=(k == 7)),
                            [(s_pe, 1)] if k == 7 else [])
                    W("scalar", s_pe)
                    O("scalar", lambda e, m1=m1, g=g, half=half, pst=pst: e.activation(
                        out=kt16[m1][:], in_=pst, func=AF.Identity,
                        scale=1.0, bias=bcol(f"b1g{g}", half * 16 + m1)),
                        [(s_act, 1)])
                if tt == 1 and p + 1 < 4:
                    rdy[(p + 1, 'a')] = load_a(p + 1, pe_gate=s_pe.count)
                W("tensor", rb[0], rb[1])
                for grp in range(2):
                    W("tensor", s_act)
                    W("tensor", s_dve, last_stt)
                    for k2 in range(16):
                        for mg in range(4):
                            m2 = grp * 4 + mg
                            O("tensor", lambda e, m2=m2, mg=mg, k2=k2: e.matmul(
                                out=PSW[:, mg * FT:(mg + 1) * FT],
                                lhsT=wtb[k2][:, m2 * 128:(m2 + 1) * 128],
                                rhs=kt16[k2][:],
                                start=(k2 == 0), stop=(k2 == 15)),
                                [(s_pe, 1)] if (k2 == 15 and mg == 3) else [])
                    W("vector", s_pe)
                    for mg in range(4):
                        m2 = grp * 4 + mg
                        O("vector", lambda e, m2=m2, mg=mg, c0=c0: e.scalar_tensor_tensor(
                            out=x_res[m2][:, c0:c0 + FT],
                            in0=PSW[:, mg * FT:(mg + 1) * FT], scalar=0.5,
                            in1=x_res[m2][:, c0:c0 + FT],
                            op0=ALU.mult, op1=ALU.add), [(s_dve, 1)])
                    last_stt = s_dve.count
            if p + 1 < 4:
                rdy[(p + 1, 'b')] = load_b(p + 1, pe_gate=s_pe.count)

    emit_ln()
    if DEBUG:
        W("sync", s_dve)
        for dd in range(8):
            O("sync", lambda e, dd=dd: e.dma_start(
                out=dbg_xln[dd], in_=x_ln[dd][:]), [(s_dbg, 16)])
    emit_ffn(0)
    if DEBUG:
        W("sync", s_dve)
        for dd in range(8):
            O("sync", lambda e, dd=dd: e.dma_start(
                out=dbg_xres[dd], in_=x_res[dd][:]), [(s_dbg, 16)])
    pe_pre_ln2 = s_pe.count
    emit_ln()

    emit_wload(lambda k: WiT_d[128 * k:128 * (k + 1), DI:2 * DI],
               8, True, wta, spill=lambda k: wsp_wiz[k], pe_gate=pe_pre_ln2)
    emit_wload(lambda k: WoT_d[128 * k:128 * (k + 1), :],
               16, False, wtb, spill=lambda k: wsp_wo[k], pe_gate=pe_pre_ln2)
    emit_wload(lambda k: WiT_d[128 * k:128 * (k + 1), 0:DI], 8, True, wta,
               pe_gate=pe_pre_ln2)

    stg4 = [wsl[0], wsl[1], wsl[2], wsl[3]]
    stg_guard = [wl_state["slot_free"].get(i) for i in range(4)]
    send_slot = {}
    CC_X = [0, 0]
    W("tensor", s_dve)
    for hh in range(2):
        for m in range(16):
            slot = stg4[m % 4]
            for tq in range(2):
                tt = hh * 2 + tq
                pst = PSA if tq % 2 == 0 else PSB
                W("tensor", s_act)
                if m >= 4:
                    psi, pv = send_slot[(m - 4, hh)]
                    W("tensor", s_st4[psi], pv)
                elif hh == 1 and m < 4:
                    psi, pv = send_slot[(12 + m, 0)]
                    W("tensor", s_st4[psi], pv)
                for k in range(8):
                    O("tensor", lambda e, m=m, k=k, tt=tt, pst=pst: e.matmul(
                        out=pst[:], lhsT=wta[k][:, m * 128:(m + 1) * 128],
                        rhs=x_ln[k][:, tt * TT:(tt + 1) * TT],
                        start=(k == 0), stop=(k == 7)),
                        [(s_pe, 1)] if k == 7 else [])
            si = m % 4
            for tq in range(2):
                pst = PSA if tq % 2 == 0 else PSB
                W("scalar", s_pe)
                W("scalar", s_dve)
                gg = stg_guard[si]
                if gg is not None:
                    W("scalar", gg[0], gg[1])
                O("scalar", lambda e, m=m, tq=tq, slot=slot, pst=pst: e.activation(
                    out=slot[:, tq * TT:(tq + 1) * TT], in_=pst[:],
                    func=AF.Identity, scale=1.0, bias=bcol("bix", m)), [(s_act, 1)])
            W("sync", s_act)
            for q in range(2):
                O("sync", lambda e, m=m, hh=hh, q=q, slot=slot: e.dma_start(
                    out=send1[hh][m // 2, 128 * (m % 2):128 * (m % 2) + 128,
                                  q * TT:(q + 1) * TT],
                    in_=slot[:, q * TT:(q + 1) * TT]), [(s_st4[si], 16)])
            if m == 0:
                O("sync", lambda e, hh=hh, slot=slot: e.dma_start(
                    out=ag_s[hh][:, :], in_=slot[0:DTR, :]), [(s_st4[si], 16)])
            send_slot[(m, hh)] = (si, s_st4[si].count)
        for si in range(4):
            W("gpsimd", s_st4[si])
        O("gpsimd", lambda e, hh=hh: e.collective_compute(
            "AllToAll", ALU.bypass, replica_groups=RG, ins=[send1[hh][:]],
            outs=[recv1[hh][:]]), [(s_cc, 1)])
        O("gpsimd", lambda e, hh=hh: e.collective_compute(
            "AllGather", ALU.bypass, replica_groups=RG, ins=[ag_s[hh][:]],
            outs=[ag_r[hh][:]]), [(s_cc, 1)])
        CC_X[hh] = s_cc.count

    W("gpsimd", s_dve, ANEG_DONE)
    W("gpsimd", s_w)
    for i in range(32):
        d, n = i // NST, i % NST
        W("gpsimd", s_gps)
        O("gpsimd", lambda e, d=d, n=n, i=i: e.tensor_scalar_mul(
            out=gdiag[i][:], in0=ident_b[:], scalar1=gvec[d][:, n:n + 1]),
            [(s_gps, 1)])
    for i in range(8):
        d, k = i // 4, i % 4
        W("gpsimd", s_gps)
        O("gpsimd", lambda e, d=d, k=k, i=i: e.tensor_scalar_mul(
            out=cdiag[i][:], in0=ident_b[:], scalar1=convw_s[d][:, k:k + 1]),
            [(s_gps, 1)])
    GD_DONE = s_gps.count

    def tmap2(b, tl):
        return tl // HALF, tl % HALF

    NIT = B * 2 * NCH
    bdc = [(b, d, c) for b in range(B) for d in range(2) for c in range(NCH)]
    gslot_free = {}
    dtmm_done = {}
    conv_done = {}
    cast_done = {}
    ccast_done = {}
    sig_done = {}
    cu_done = {}
    v_done = {}
    scan_last = {}
    dt_done = {}
    ymm_done = {}
    yevac_done = {}
    st_done = {}

    cx_done = {}

    def mx_dma(it):
        b, d, c = bdc[it]
        p4 = it % 4
        t0 = c * TC
        src, off = tmap2(b, t0)
        W("sync", s_cc, CC_X[b])
        if it >= 4:
            W("sync", s_act, ccast_done[it - 4])
            W("sync", s_act, cast_done[it - 4])
        if c == 0:
            W("gpsimd", s_cc, CC_X[b])
            O("gpsimd", lambda e, p4=p4: e.memset(cxb[p4][:, 0:3], 0.0),
              [(s_gps, 1)])
        else:
            psrc, poff = tmap2(b, t0 - 3)
            O("sync", lambda e, p4=p4, b=b, psrc=psrc, poff=poff, d=d: e.dma_start(
                out=cxb[p4][:, 0:3],
                in_=recv1[b][psrc, 128 * d:128 * d + 128, poff:poff + 3]),
                [(s_cx[it % 2], 16)])
        O("sync", lambda e, p4=p4, b=b, src=src, off=off, d=d: e.dma_start(
            out=cxb[p4][:, 3:3 + TC],
            in_=recv1[b][src, 128 * d:128 * d + 128, off:off + TC]),
            [(s_cx[it % 2], 16)])
        O("sync", lambda e, b=b, src=src, off=off, p4=p4: e.dma_start(
            out=x32f[p4][:], in_=ag_r[b][src, :, off:off + TC]),
            [(s_cx[it % 2], 16)])
        cx_done[it] = s_cx[it % 2].count

    def mx_front(it):
        b, d, c = bdc[it]
        pp = it % 2
        p4 = it % 4
        W("scalar", s_cx[pp], cx_done[it])
        if it >= 2:
            W("scalar", s_pe, dtmm_done.get(it - 2))
        O("scalar", lambda e, pp=pp, p4=p4: e.activation(
            out=x32b[pp][:], in_=x32f[p4][:], func=AF.Copy, scale=1.0),
            [(s_act, 1)])
        cast_done[it] = s_act.count
        pdt = PSA if pp == 0 else PSB
        W("tensor", s_act, cast_done[it])
        O("tensor", lambda e, pp=pp, d=d, pdt=pdt: e.matmul(
            out=pdt[:, 0:TC], lhsT=wdtT[:, 128 * d:128 * d + 128],
            rhs=x32b[pp][:], start=True, stop=True), [(s_pe, 1)])
        dtmm_done[it] = s_pe.count
        W("scalar", s_pe, dtmm_done[it])
        O("scalar", lambda e, d=d, pdt=pdt: e.activation(
            out=ebuf[:], in_=pdt[:, 0:TC], func=AF.Exp, scale=1.0,
            bias=bcol("bdt", d)), [(s_act, 1)])
        if it >= 2:
            W("scalar", s_gps, v_done[it - 2])
        O("scalar", lambda e, pp=pp: e.activation(
            out=dtb[pp][:], in_=ebuf[:], func=AF.Ln, scale=1.0,
            bias=bcol("one")), [(s_act, 1)])
        dt_done[it] = s_act.count
        if it >= 2:
            W("scalar", s_pe, conv_done.get(it - 2))
        O("scalar", lambda e, pp=pp, p4=p4: e.activation(
            out=cxbb[pp][:], in_=cxb[p4][:], func=AF.Copy, scale=1.0),
            [(s_act, 1)])
        ccast_done[it] = s_act.count
        W("tensor", s_cc, CC_X[b])
        W("tensor", s_act, ccast_done[it])
        if it >= 2:
            W("tensor", s_act, sig_done[it - 2])
        W("tensor", s_gps, GD_DONE)
        for k in range(4):
            O("tensor", lambda e, pp=pp, d=d, k=k: e.matmul(
                out=PSW[:, pp * TC:(pp + 1) * TC], lhsT=cdiag[d * 4 + k][:],
                rhs=cxbb[pp][:, k:k + TC],
                start=(k == 0), stop=(k == 3)),
                [(s_pe, 1)] if k == 3 else [])
        conv_done[it] = s_pe.count

    def mx_gate(it):
        b, d, c = bdc[it]
        pp = it % 2
        W("scalar", s_pe, conv_done[it])
        if it >= 2:
            W("scalar", s_gps, v_done[it - 2])
        O("scalar", lambda e, pp=pp, d=d: e.activation(
            out=sigb[pp][:], in_=PSW[:, pp * TC:(pp + 1) * TC], func=AF.Silu,
            scale=1.0, bias=bcol("convb", d)), [(s_act, 1)])
        sig_done[it] = s_act.count
        W("gpsimd", s_act, sig_done[it])
        if it >= 2:
            W("gpsimd", s_dve, scan_last[it - 2])
        O("gpsimd", lambda e, pp=pp: e.tensor_mul(
            out=vbuf[pp][:], in0=sigb[pp][:], in1=dtb[pp][:]),
            [(s_gps, 1)])
        v_done[it] = s_gps.count

    def mx_back(it):
        b, d, c = bdc[it]
        pp = it % 2
        t0 = c * TC
        for n in range(NST):
            sl = n % NGS
            gf = gslot_free.get(sl)
            if gf is not None:
                W("scalar", s_dve, gf)
            W("scalar", s_act, dt_done[it])
            O("scalar", lambda e, n=n, pp=pp, d=d, sl=sl: e.activation(
                out=gbuf[:, sl * TC:(sl + 1) * TC], in_=dtb[pp][:],
                func=AF.Exp, scale=aneg[d][:, n:n + 1]), [(s_act, 1)])
            if n == 5 and it + 1 < NIT:
                mx_gate(it + 1)
            W("vector", s_act)
            W("vector", s_gps, v_done[it])
            if it >= 2:
                W("vector", s_pe, ymm_done[it - 2])
            init = 0.0 if c == 0 else \
                hbuf[1 - pp][:, n * TC + TC - 1:n * TC + TC]
            O("vector", lambda e, n=n, pp=pp, sl=sl, init=init:
              e.tensor_tensor_scan(
                out=hbuf[pp][:, n * TC:(n + 1) * TC],
                data0=gbuf[:, sl * TC:(sl + 1) * TC],
                data1=vbuf[pp][:], initial=init,
                op0=ALU.mult, op1=ALU.add), [(s_dve, 1)])
            gslot_free[sl] = s_dve.count
            W("tensor", s_dve)
            if n == 0 and it >= 2:
                W("tensor", s_act, yevac_done[it - 2])
            W("tensor", s_gps, GD_DONE)
            O("tensor", lambda e, n=n, pp=pp, d=d: e.matmul(
                out=PSY[pp][:], lhsT=gdiag[d * NST + n][:],
                rhs=hbuf[pp][:, n * TC:(n + 1) * TC],
                start=(n == 0), stop=(n == NST - 1)),
                [(s_pe, 1)] if n == NST - 1 else [])
        scan_last[it] = s_dve.count
        ymm_done[it] = s_pe.count
        W("scalar", s_pe, ymm_done[it])
        if it >= 2:
            W("scalar", s_yst, st_done[it - 2])
        O("scalar", lambda e, pp=pp: e.activation(
            out=ysb[pp][:], in_=PSY[pp][:], func=AF.Copy, scale=1.0),
            [(s_act, 1)])
        yevac_done[it] = s_act.count
        W("sync", s_act, yevac_done[it])
        dst, dcol = tmap2(b, t0)
        O("sync", lambda e, pp=pp, b=b, dst=dst, dcol=dcol, d=d: e.dma_start(
            out=ysend[b][dst, 128 * d:128 * d + 128, dcol:dcol + TC],
            in_=ysb[pp][:]), [(s_yst, 16)])
        st_done[it] = s_yst.count

    CC_Y = [0, 0]
    mx_dma(0)
    mx_dma(1)
    mx_front(0)
    mx_gate(0)
    for it in range(NIT):
        if it + 2 < NIT:
            mx_dma(it + 2)
        if it + 1 < NIT:
            mx_front(it + 1)
        mx_back(it)
        if it == NIT // 2 - 1:
            W("gpsimd", s_yst, st_done[it])
            O("gpsimd", lambda e: e.collective_compute(
                "AllToAll", ALU.bypass, replica_groups=RG, ins=[ysend[0][:]],
                outs=[yrecv[0][:]]), [(s_cc, 1)])
            CC_Y[0] = s_cc.count

    W("gpsimd", s_yst)
    O("gpsimd", lambda e: e.collective_compute(
        "AllToAll", ALU.bypass, replica_groups=RG, ins=[ysend[1][:]],
        outs=[yrecv[1][:]]), [(s_cc, 1)])
    CC_Y[1] = s_cc.count

    W("sync", s_pe)
    W("sync", s_yst)
    for kt in range(8):
        O("sync", lambda e, kt=kt: e.dma_start(out=wta[kt][:], in_=wsp_wiz[kt]),
          [(s_wl, 16)])
    for kt in range(16):
        O("sync", lambda e, kt=kt: e.dma_start(out=wtb[kt][:], in_=wsp_wo[kt]),
          [(s_wl, 16)])
    WL_P6 = s_wl.count
    for d in range(8):
        W("vector", s_dve)
        O("vector", lambda e, d=d: e.tensor_scalar_add(
            out=x_res[d][:], in0=x_res[d][:], scalar1=bcol("bo", d)), [(s_dve, 1)])
        W("vector", s_dve)
        O("vector", lambda e, d=d: e.scalar_tensor_tensor(
            out=x_res[d][:], in0=x_ln[d][:], scalar=bcol("dvm"),
            in1=x_res[d][:], op0=ALU.mult, op1=ALU.add), [(s_dve, 1)])
    W("tensor", s_dve)
    gate_done = {}
    for tt in range(4):
        for k in range(16):
            pp = k % 4
            pst = (PSA if k % 2 == 0 else PSB)[:, 0:FT]
            W("tensor", s_wl, WL_P6)
            W("tensor", s_act)
            for kk in range(8):
                O("tensor", lambda e, k=k, kk=kk, tt=tt, pst=pst: e.matmul(
                    out=pst, lhsT=wta[kk][:, k * 128:(k + 1) * 128],
                    rhs=x_ln[kk][:, tt * FT:(tt + 1) * FT],
                    start=(kk == 0), stop=(kk == 7)),
                    [(s_pe, 1)] if kk == 7 else [])
            W("sync", s_cc, CC_Y[tt // 2])
            idx = tt * 16 + k
            if idx >= 4:
                W("sync", s_dve, gate_done.get(idx - 4, 0))
            O("sync", lambda e, k=k, tt=tt, pp=pp: e.dma_start(
                out=yld[pp][:],
                in_=yrecv[tt // 2][k // 2, 128 * (k % 2):128 * (k % 2) + 128,
                                   (tt % 2) * FT:(tt % 2 + 1) * FT]),
                [(s_yld[pp], 16)])
            ycount = s_yld[pp].count
            W("scalar", s_pe)
            W("scalar", s_dve)
            O("scalar", lambda e, k=k, pst=pst: e.activation(
                out=szb[:], in_=pst, func=AF.Silu, scale=1.0,
                bias=bcol("biz", k)), [(s_act, 1)])
            W("vector", s_act)
            W("vector", s_yld[pp], ycount)
            W("vector", s_pe)
            O("vector", lambda e, k=k, pp=pp: e.tensor_mul(
                out=kt16[k][:], in0=yld[pp][:], in1=szb[:]), [(s_dve, 1)])
            gate_done[idx] = s_dve.count
        for grp in range(2):
            W("tensor", s_dve)
            for k2 in range(16):
                for mg in range(4):
                    m2 = grp * 4 + mg
                    O("tensor", lambda e, m2=m2, mg=mg, k2=k2: e.matmul(
                        out=PSW[:, mg * FT:(mg + 1) * FT],
                        lhsT=wtb[k2][:, m2 * 128:(m2 + 1) * 128],
                        rhs=kt16[k2][:], start=(k2 == 0), stop=(k2 == 15)),
                        [(s_pe, 1)] if (k2 == 15 and mg == 3) else [])
            W("vector", s_pe)
            for mg in range(4):
                m2 = grp * 4 + mg
                O("vector", lambda e, m2=m2, mg=mg, tt=tt: e.scalar_tensor_tensor(
                    out=x_res[m2][:, tt * FT:(tt + 1) * FT],
                    in0=PSW[:, mg * FT:(mg + 1) * FT], scalar=1.0,
                    in1=x_res[m2][:, tt * FT:(tt + 1) * FT],
                    op0=ALU.mult, op1=ALU.add), [(s_dve, 1)])

    emit_ln()
    emit_ffn(1)

    W("sync", s_dve)
    for d in range(8):
        O("sync", lambda e, d=d: e.dma_start(
            out=out_d[d], in_=x_res[d][:]), [(s_out, 16)])
    W("sync", s_out)
    if DEBUG:
        W("sync", s_dbg)

    from contextlib import ExitStack
    with ExitStack() as es:
        for s in p.sems.values():
            s.h = es.enter_context(nc.semaphore(s.name))
        block = es.enter_context(nc.Block())
        for ename in ENGS:
            ops = list(p.streams[ename])

            def emit_stream(e, ops=ops):
                for fn in ops:
                    fn(e)
            getattr(block, ename)(emit_stream)
    return nc


_NC_CACHE = None


def kernel(**inputs):
    global _NC_CACHE
    f32 = lambda a: np.ascontiguousarray(np.asarray(a, dtype=np.float32))
    x = f32(inputs["L0_attn_x"])
    Wi = f32(inputs["L0_attn_Wi"]); bi = f32(inputs["L0_attn_bi"])
    Wo = f32(inputs["L0_attn_Wo"]); bo = f32(inputs["L0_attn_bo"])
    Wdt = f32(inputs["L0_attn_Wdt"]); bdt = f32(inputs["L0_attn_bdt"])
    convw = f32(inputs["L0_attn_conv_w"])[:, 0, :]
    convb = f32(inputs["L0_attn_conv_b"])
    Alog = f32(inputs["L0_attn_A_log"]); Bm = f32(inputs["L0_attn_Bm"])
    Cm = f32(inputs["L0_attn_Cm"]); Dv = f32(inputs["L0_attn_Dv"])
    W1 = f32(inputs["L0_ffn_W1"]); b1 = f32(inputs["L0_ffn_b1"])
    W2 = f32(inputs["L0_ffn_W2"]); b2 = f32(inputs["L0_ffn_b2"])

    if _NC_CACHE is None:
        _NC_CACHE = build_nc()
    nc = _NC_CACHE

    WiT = np.ascontiguousarray(Wi.T)
    WoT = np.ascontiguousarray(Wo.T)
    W1T = {}
    W2T = {}
    for g in set([c // 2 for c in range(NCORES)] + [4 + c // 2 for c in range(NCORES)]):
        W1T[g] = np.ascontiguousarray(W1[g].T)
        W2T[g] = np.ascontiguousarray(W2[g].T)

    in_maps = []
    for c in range(NCORES):
        g0, g1 = c // 2, 4 + c // 2
        lo = 256 * c
        x_blk = np.concatenate([x[0, HALF * c:HALF * (c + 1)],
                                x[1, HALF * c:HALF * (c + 1)]], axis=0)
        biascat = np.concatenate([
            b1[g0].reshape(32, 128).T, b1[g1].reshape(32, 128).T,
            b2[g0].reshape(8, 128).T, b2[g1].reshape(8, 128).T,
            bi[:DI].reshape(16, 128).T, bi[DI:].reshape(16, 128).T,
            bo.reshape(8, 128).T,
            bdt[lo:lo + MYCH].reshape(2, 128).T,
            convb[lo:lo + MYCH].reshape(2, 128).T,
        ], axis=1)
        assert biascat.shape == (128, NB_DMA)
        in_maps.append(dict(
            xT_blk=np.ascontiguousarray(x_blk.T),
            WiT=WiT, WoT=WoT,
            W1Tg0=W1T[g0], W2Tg0=W2T[g0],
            W1Tg1=W1T[g1], W2Tg1=W2T[g1],
            WdtT_my=np.ascontiguousarray(Wdt[lo:lo + MYCH].T),
            biascat=np.ascontiguousarray(biascat),
            convw_my=np.ascontiguousarray(convw[lo:lo + MYCH]),
            Alog_my=np.ascontiguousarray(Alog[lo:lo + MYCH]),
            Bm_my=np.ascontiguousarray(Bm[lo:lo + MYCH]),
            Cm_my=np.ascontiguousarray(Cm[lo:lo + MYCH]),
            Dv=Dv,
        ))
    res = run_bass_kernel_spmd(nc, in_maps, core_ids=list(range(NCORES)))
    out = np.empty((B, T, DM), np.float32)
    for c in range(NCORES):
        o = res.results[c]["out"].reshape(DM, NT)
        out[0, HALF * c:HALF * (c + 1)] = o[:, :HALF].T
        out[1, HALF * c:HALF * (c + 1)] = o[:, HALF:].T
    return out


if __name__ == "__main__":
    import reference as refmod
    import jax
    with jax.default_device(jax.devices("cpu")[0]):
        inp = refmod.setup_inputs(0)
        ref = np.asarray(refmod.reference(**inp))
        inp_np = {k: np.asarray(v) for k, v in inp.items()}
    got = kernel(**inp_np)
    d = got - ref
    rvr = float((d ** 2).mean() / (ref.astype(np.float64) ** 2).mean())
    print("rvr:", rvr, "maxabs:", np.abs(d).max())


# version 25 (final)
# speedup vs baseline: 1.0138x; 1.0138x over previous
import os
os.environ.setdefault("TRN_ALLOW_ICI_ALL_TO_ALL", "1")

import numpy as np
import concourse.bass as bass
import concourse.mybir as mybir
from concourse.bass_utils import run_bass_kernel_spmd

F32 = mybir.dt.float32
BF16 = mybir.dt.bfloat16
AF = mybir.ActivationFunctionType
ALU = mybir.AluOpType
AX = mybir.AxisListType

NCORES = 8
B = 2
T = 8192
DM = 1024
DI = 2048
NST = 16
DTR = 32
NT = 2048
HALF = 1024
MYCH = 256
TC = 512
NCH = T // TC
TT = 512
TTL = 512
RG = [list(range(NCORES))]
KB = 1024

ENGS = ("sync", "gpsimd", "vector", "scalar", "tensor")
DEBUG = False

BV = {"dvm": 0, "b1g0": 1, "b1g1": 33, "b2g0": 65, "b2g1": 73,
      "bix": 81, "biz": 97, "bo": 113, "bdt": 121, "convb": 123,
      "hb2g0": 125, "hb2g1": 133, "eps": 141, "one": 142}
NB_DMA = 124


class Sem:
    __slots__ = ("name", "count", "h")

    def __init__(self, name):
        self.name, self.count, self.h = name, 0, None


class Prog:
    def __init__(self):
        self.streams = {e: [] for e in ENGS}
        self.sems = {}
        self.marks = {e: {} for e in ENGS}

    def sem(self, name):
        if name not in self.sems:
            self.sems[name] = Sem(name)
        return self.sems[name]

    def wait(self, eng, sem, val=None):
        v = sem.count if val is None else val
        if v <= 0 or self.marks[eng].get(sem.name, -1) >= v:
            return
        self.marks[eng][sem.name] = v
        self.streams[eng].append(lambda e, s=sem, v=v: e.wait_ge(s.h, v))

    def op(self, eng, fn, incs=()):
        for s, n in incs:
            s.count += n

        def clo(e, fn=fn, incs=tuple(incs)):
            inst = fn(e)
            for s, n in incs:
                inst.then_inc(s.h, n)
            return inst
        self.streams[eng].append(clo)


def build_nc():
    nc = bass.Bass("TRN2", target_bir_lowering=False)
    p = Prog()
    W = p.wait
    O = p.op

    def din(name, shape):
        return nc.dram_tensor(name, list(shape), F32, kind="ExternalInput").ap()

    x_d = din("xT_blk", (DM, NT))
    WiT_d = din("WiT", (DM, 2 * DI))
    WoT_d = din("WoT", (DI, DM))
    W1T_d = [din(f"W1Tg{i}", (DM, 4096)) for i in range(2)]
    W2T_d = [din(f"W2Tg{i}", (4096, DM)) for i in range(2)]
    WdtT_d = din("WdtT_my", (DTR, MYCH))
    biascat_d = din("biascat", (128, NB_DMA))
    convw_d = din("convw_my", (MYCH, 4))
    Alog_d = din("Alog_my", (MYCH, NST))
    Bm_d = din("Bm_my", (MYCH, NST))
    Cm_d = din("Cm_my", (MYCH, NST))
    Dv_d = din("Dv", (DI,))
    out_d = nc.dram_tensor("out", [8, 128, NT], F32, kind="ExternalOutput").ap()
    if DEBUG:
        dbg_xln = nc.dram_tensor("dbg_xln", [8, 128, NT], BF16,
                                 kind="ExternalOutput").ap()
        dbg_xres = nc.dram_tensor("dbg_xres", [8, 128, NT], F32,
                                  kind="ExternalOutput").ap()
        s_dbg = p.sem("s_dbg")

    send1 = [nc.dram_tensor(f"send1{b}", [NCORES, MYCH, HALF], F32,
                            kind="Internal").ap() for b in range(2)]
    recv1 = [nc.dram_tensor(f"recv1{b}", [NCORES, MYCH, HALF], F32,
                            kind="Internal").ap() for b in range(2)]
    ag_s = [nc.dram_tensor(f"ag_s{b}", [DTR, HALF], F32,
                           kind="Internal").ap() for b in range(2)]
    ag_r = [nc.dram_tensor(f"ag_r{b}", [NCORES, DTR, HALF], F32,
                           kind="Internal").ap() for b in range(2)]
    ysend = [nc.dram_tensor(f"ysend{b}", [NCORES, MYCH, HALF], F32,
                            kind="Internal").ap() for b in range(2)]
    yrecv = [nc.dram_tensor(f"yrecv{b}", [NCORES, MYCH, HALF], F32,
                            kind="Internal").ap() for b in range(2)]
    wsp_w1 = nc.dram_tensor("wsp_w1", [2, 2, 8, 128, 2048], BF16, kind="Internal").ap()
    wsp_w2 = nc.dram_tensor("wsp_w2", [2, 2, 16, 128, 1024], BF16, kind="Internal").ap()
    wsp_wiz = nc.dram_tensor("wsp_wiz", [8, 128, 2048], BF16, kind="Internal").ap()
    wsp_wo = nc.dram_tensor("wsp_wo", [16, 128, 1024], BF16, kind="Internal").ap()

    def S(name, shape, dt=F32):
        return nc.alloc_sbuf_tensor(name, list(shape), dt).ap()

    x_res = [S(f"xres{i}", (128, NT)) for i in range(8)]
    x_ln = [S(f"xln{i}", (128, NT), BF16) for i in range(8)]
    biasv = S("biasv", (128, 144))
    ident_f = S("ident_f", (128, 128))
    ident_b = S("ident_b", (128, 128), BF16)
    ones_colf = S("ones_colf", (128, 1))
    ones_colb = S("ones_colb", (128, 1), BF16)
    ones_rowf = S("ones_rowf", (1, 128))
    aneg = [S(f"aneg{d}", (128, NST)) for d in range(2)]
    gvec = [S(f"gvec{d}", (128, NST)) for d in range(2)]
    convw_s = [S(f"convw{d}", (128, 4)) for d in range(2)]
    cst16 = [S(f"cst16_{i}", (128, NST)) for i in range(2)]
    dv16 = S("dv16", (128, NST))
    wdtT = S("wdtT", (DTR, MYCH), BF16)
    wdtstg = S("wdtstg", (DTR, MYCH))

    POOL_KB = 96
    pool = nc.alloc_sbuf_tensor("pool", [128, POOL_KB * KB], mybir.dt.uint8)
    pool_base = nc.lookup_mloc(pool).addr

    def V(name, shape, dt, off_bytes):
        per_part = int(np.prod(shape[1:])) * mybir.dt.size(dt)
        assert off_bytes + per_part <= POOL_KB * KB, (name, off_bytes, per_part)
        h = nc.virtual_sbuf_tensor(name, list(shape), dt)
        nc._tensor(h.name, h.shape, h.dtype, type="SB")
        mloc = nc.lookup_mloc(h)
        mloc.allocated = True
        mloc.addr = pool_base + off_bytes
        return h.ap()

    FT = 512
    wta = [V(f"wta{i}", (128, 2048), BF16, (0 + 4 * i) * KB) for i in range(8)]
    wtb = [V(f"wtb{i}", (128, 1024), BF16, (32 + 2 * i) * KB) for i in range(16)]
    kt16 = [V(f"kt16_{i}", (128, FT), BF16, (64 + i) * KB) for i in range(16)]
    wsl = [V(f"wsl{i}", (128, 1024), F32, (80 + 4 * i) * KB) for i in range(4)]
    wslp = [V(f"wslp{i}", (128, 2048), F32, (80 + 8 * i) * KB) for i in range(2)]
    sqb = [V(f"sqb{i}", (128, TTL), BF16, (64 + i) * KB) for i in range(2)]
    xcb = [V(f"xcb{i}", (128, TTL), BF16, (66 + i) * KB) for i in range(2)]
    lntmp = V("lntmp", (128, TTL), F32, 68 * KB)
    lnA = V("lnA", (1, TTL), F32, 70 * KB)
    lnB = V("lnB", (1, TTL), F32, 72 * KB)
    lnC = V("lnC", (1, TTL), F32, 74 * KB)
    NGS = 4
    gbuf = V("gbuf", (128, NGS * TC), F32, 0 * KB)
    hbuf = [V(f"hbuf{i}", (128, NST * TC), BF16, (8 + 16 * i) * KB) for i in range(2)]
    mx = 40 * KB

    def MV(name, shape, dt):
        nonlocal mx
        per_part = int(np.prod(shape[1:])) * mybir.dt.size(dt)
        ap = V(name, shape, dt, mx)
        mx += (per_part + 31) // 32 * 32
        return ap

    ysb = [MV(f"ysb{i}", (128, TC), F32) for i in range(2)]
    ctmp = MV("ctmp", (128, TC), F32)
    cxb = [MV(f"cx{i}", (128, TC + 3), F32) for i in range(4)]
    dtb = [MV(f"dt{i}", (128, TC), F32) for i in range(2)]
    cacc = [MV(f"cacc{i}", (128, TC), F32) for i in range(2)]
    sigb = [MV(f"sig{i}", (128, TC), F32) for i in range(2)]
    vbuf = [MV(f"v{i}", (128, TC), F32) for i in range(2)]
    ebuf = MV("ebuf", (128, TC), F32)
    x32f = [MV(f"x32f{i}", (DTR, TC), F32) for i in range(4)]
    x32b = [MV(f"x32b{i}", (DTR, TC), BF16) for i in range(2)]
    gdiag = [MV(f"gdt{i}", (128, 128), BF16) for i in range(32)]
    assert mx <= 96 * KB, mx
    yld = [V(f"yld{i}", (128, FT), F32, (80 + 2 * i) * KB) for i in range(4)]
    szb = V("szb", (128, FT), F32, 88 * KB)

    PSA = nc.alloc_psum_tensor("psa", [128, 512], F32).ap()
    PSB = nc.alloc_psum_tensor("psb", [128, 512], F32).ap()
    PSW = nc.alloc_psum_tensor("psw", [128, 2048], F32).ap()
    PSYx = nc.alloc_psum_tensor("psy", [128, 2 * TC], F32).ap()
    PSY = [PSYx[:, 0:TC], PSYx[:, TC:2 * TC]]

    s_pe = p.sem("s_pe")
    s_act = p.sem("s_act")
    s_dve = p.sem("s_dve")
    s_gps = p.sem("s_gps")
    s_x = p.sem("s_x")
    s_w = p.sem("s_w")
    s_wr = p.sem("s_wr")
    s_sp = p.sem("s_sp")
    s_wl = p.sem("s_wl")
    s_cc = p.sem("s_cc")
    s_cx = [p.sem("s_cx0"), p.sem("s_cx1")]
    s_yst = p.sem("s_yst")
    s_yld = [p.sem(f"s_yld{i}") for i in range(4)]
    s_st4 = [p.sem(f"s_st4_{i}") for i in range(4)]
    s_out = p.sem("s_out")

    def bcol(tag, i=0):
        return biasv[:, BV[tag] + i:BV[tag] + i + 1]

    O("gpsimd", lambda e: e.memset(ident_f[:], 0.0), [(s_gps, 1)])
    W("gpsimd", s_gps)
    O("gpsimd", lambda e: e.affine_select(
        out=ident_f[:], in_=ident_f[:], compare_op=ALU.not_equal, fill=1.0,
        base=0, pattern=[[-1, 128]], channel_multiplier=1), [(s_gps, 1)])
    W("gpsimd", s_gps)
    O("gpsimd", lambda e: e.tensor_copy(out=ident_b[:], in_=ident_f[:]), [(s_gps, 1)])
    O("gpsimd", lambda e: e.memset(ones_colf[:], 1.0), [(s_gps, 1)])
    O("gpsimd", lambda e: e.memset(ones_colb[:], 1.0), [(s_gps, 1)])
    O("gpsimd", lambda e: e.memset(ones_rowf[:], 1.0), [(s_gps, 1)])
    O("gpsimd", lambda e: e.memset(biasv[:, BV["eps"]:BV["eps"] + 1], 1e-5),
      [(s_gps, 1)])
    O("gpsimd", lambda e: e.memset(biasv[:, BV["one"]:BV["one"] + 1], 1.0),
      [(s_gps, 1)])
    GPS_CONST = s_gps.count

    for d in range(8):
        O("sync", lambda e, d=d: e.dma_start(
            out=x_res[d][:], in_=x_d[d * 128:(d + 1) * 128, :]), [(s_x, 16)])

    def ld(dst, src, sem=s_w):
        O("sync", lambda e, dst=dst, src=src: e.dma_start(out=dst, in_=src),
          [(sem, 16)])

    ld(biasv[:, 1:1 + NB_DMA], biascat_d[:, :])
    ld(cst16[0][:], Bm_d[0:128, :])
    ld(cst16[1][:], Bm_d[128:256, :])
    ld(aneg[0][:], Cm_d[0:128, :])
    ld(aneg[1][:], Cm_d[128:256, :])
    ld(dv16[:], Dv_d.rearrange("(a b) -> a b", a=128))
    ld(convw_s[0][:], convw_d[0:128, :])
    ld(convw_s[1][:], convw_d[128:256, :])
    ld(wdtstg[:], WdtT_d[:, :])
    W("vector", s_w)
    for d in range(2):
        O("vector", lambda e, d=d: e.tensor_mul(
            out=gvec[d][:], in0=cst16[d][:], in1=aneg[d][:]), [(s_dve, 1)])
    GV_DONE = s_dve.count
    W("sync", s_dve, GV_DONE)
    ld(cst16[0][:], Alog_d[0:128, :])
    ld(cst16[1][:], Alog_d[128:256, :])
    W("scalar", s_w)
    for d in range(2):
        O("scalar", lambda e, d=d: e.activation(
            out=cst16[d][:], in_=cst16[d][:], func=AF.Exp, scale=1.0),
            [(s_act, 1)])
    O("scalar", lambda e: e.activation(
        out=wdtT[:], in_=wdtstg[:], func=AF.Copy, scale=1.0), [(s_act, 1)])
    W("vector", s_act)
    for d in range(2):
        O("vector", lambda e, d=d: e.tensor_scalar_mul(
            out=aneg[d][:], in0=cst16[d][:], scalar1=-1.0), [(s_dve, 1)])
    ANEG_DONE = s_dve.count
    for g in range(2):
        O("vector", lambda e, g=g: e.tensor_scalar_mul(
            out=biasv[:, BV[f"hb2g{g}"]:BV[f"hb2g{g}"] + 8],
            in0=biasv[:, BV[f"b2g{g}"]:BV[f"b2g{g}"] + 8], scalar1=0.5),
            [(s_dve, 1)])

    O("vector", lambda e: e.reduce_sum(
        out=dv16[:, 0:1], in_=dv16[:], axis=AX.X), [(s_dve, 1)])
    W("tensor", s_dve)
    W("tensor", s_gps, GPS_CONST)
    O("tensor", lambda e: e.transpose(
        out=PSA[0:1, 0:128], in_=dv16[:, 0:1],
        identity=ident_f[:]), [(s_pe, 1)])
    W("vector", s_pe)
    O("vector", lambda e: e.reduce_sum(
        out=wdtstg[0:1, 0:1], in_=PSA[0:1, 0:128], axis=AX.X), [(s_dve, 1)])
    W("tensor", s_dve)
    O("tensor", lambda e: e.matmul(
        out=PSB[0:128, 0:1], lhsT=ones_rowf[:], rhs=wdtstg[0:1, 0:1],
        start=True, stop=True), [(s_pe, 1)])
    W("scalar", s_pe)
    O("scalar", lambda e: e.activation(
        out=biasv[:, 0:1], in_=PSB[0:128, 0:1], func=AF.Copy, scale=1.0 / DI),
        [(s_act, 1)])

    ln_state = {"first": True}

    def emit_ln():
        grp_done = []
        for g in range(NT // TTL):
            c0 = g * TTL
            sq_pe = {}
            for d in range(8):
                pp = d % 2
                if ln_state["first"]:
                    W("scalar", s_x)
                    W("vector", s_x)
                W("scalar", s_dve)
                W("scalar", s_pe, sq_pe.get(d - 2))
                O("scalar", lambda e, d=d, pp=pp, c0=c0: e.activation(
                    out=sqb[pp][:], in_=x_res[d][:, c0:c0 + TTL],
                    func=AF.Square, scale=1.0), [(s_act, 1)])
                W("vector", s_pe, sq_pe.get(d - 2))
                O("vector", lambda e, d=d, pp=pp, c0=c0: e.tensor_copy(
                    out=xcb[pp][:], in_=x_res[d][:, c0:c0 + TTL]), [(s_dve, 1)])
                if ln_state["first"]:
                    W("tensor", s_gps, GPS_CONST)
                W("tensor", s_dve)
                W("tensor", s_act)
                O("tensor", lambda e, d=d, pp=pp: e.matmul(
                    out=PSA[0:1, 0:TTL], lhsT=ones_colb[:], rhs=xcb[pp][:],
                    start=(d == 0), stop=(d == 7)), [])
                O("tensor", lambda e, d=d, pp=pp: e.matmul(
                    out=PSB[0:1, 0:TTL], lhsT=ones_colb[:], rhs=sqb[pp][:],
                    start=(d == 0), stop=(d == 7)), [(s_pe, 1)])
                sq_pe[d] = s_pe.count
            ln_state["first"] = False
            W("vector", s_pe)
            O("vector", lambda e: e.tensor_scalar_mul(
                out=lnA[0:1, :], in0=PSA[0:1, 0:TTL], scalar1=1.0 / DM),
                [(s_dve, 1)])
            W("vector", s_dve)
            O("vector", lambda e: e.tensor_scalar_mul(
                out=lnB[0:1, :], in0=PSB[0:1, 0:TTL], scalar1=1.0 / DM),
                [(s_dve, 1)])
            W("vector", s_dve)
            O("vector", lambda e: e.tensor_mul(
                out=lnC[0:1, :], in0=lnA[0:1, :], in1=lnA[0:1, :]),
                [(s_dve, 1)])
            W("vector", s_dve)
            O("vector", lambda e: e.tensor_sub(
                out=lnB[0:1, :], in0=lnB[0:1, :], in1=lnC[0:1, :]),
                [(s_dve, 1)])
            W("scalar", s_dve)
            O("scalar", lambda e: e.activation(
                out=lnC[0:1, :], in_=lnB[0:1, :], func=AF.Sqrt,
                scale=1.0, bias=biasv[0:1, BV["eps"]:BV["eps"] + 1]),
                [(s_act, 1)])
            W("vector", s_act)
            O("vector", lambda e: e.reciprocal(
                out=lnB[0:1, :], in_=lnC[0:1, :]), [(s_dve, 1)])
            W("tensor", s_dve)
            O("tensor", lambda e: e.matmul(
                out=PSA[:, 0:TTL], lhsT=ones_rowf[:], rhs=lnA[0:1, :],
                start=True, stop=True), [(s_pe, 1)])
            O("tensor", lambda e: e.matmul(
                out=PSB[:, 0:TTL], lhsT=ones_rowf[:], rhs=lnB[0:1, :],
                start=True, stop=True), [(s_pe, 1)])
            W("vector", s_pe)
            for d in range(8):
                O("vector", lambda e, d=d, c0=c0: e.tensor_sub(
                    out=lntmp[:], in0=x_res[d][:, c0:c0 + TTL],
                    in1=PSA[:, 0:TTL]), [(s_dve, 1)])
                W("vector", s_dve)
                O("vector", lambda e, d=d, c0=c0: e.tensor_mul(
                    out=x_ln[d][:, c0:c0 + TTL], in0=lntmp[:],
                    in1=PSB[:, 0:TTL]), [(s_dve, 1)])
                W("vector", s_dve)
            grp_done.append(s_dve.count)
        return grp_done

    wl_state = {"slot_free": {}}

    def emit_wload(src_rows, ntiles, wide, dst, spill=None, pe_gate=None):
        eng = "vector"
        esem = s_dve
        nslot = 2 if wide else 4
        sp0 = s_sp.count
        pe0 = s_pe.count if pe_gate is None else pe_gate
        dma_done = {}
        cast_done = {}
        for k in range(ntiles):
            si = k % nslot
            slot = wslp[si] if wide else wsl[si]
            for sk in ([2 * si, 2 * si + 1] if wide else [si]):
                fr = wl_state["slot_free"].get(sk)
                if fr is not None:
                    W("sync", fr[0], fr[1])
            if k - nslot in cast_done:
                W("sync", esem, cast_done[k - nslot])
            O("sync", lambda e, k=k, slot=slot: e.dma_start(
                out=slot[:], in_=src_rows(k)), [(s_wr, 16)])
            dma_done[k] = s_wr.count
            W(eng, s_wr, dma_done[k])
            W(eng, s_pe, pe0)
            W(eng, s_sp, sp0)
            O(eng, lambda e, k=k, slot=slot: e.tensor_copy(
                out=dst[k][:], in_=slot[:]), [(esem, 1)])
            cast_done[k] = esem.count
            for sk in ([2 * si, 2 * si + 1] if wide else [si]):
                wl_state["slot_free"][sk] = (esem, cast_done[k])
            if spill is not None:
                W("sync", esem, cast_done[k])
                O("sync", lambda e, k=k: e.dma_start(
                    out=spill(k), in_=dst[k][:, :]), [(s_sp, 16)])
        return (esem, cast_done[ntiles - 1])

    def emit_ffn(which, pg_first=None):
        for g in range(2):
            for d in range(8):
                W("vector", s_pe)
                W("vector", s_act)
                W("vector", s_dve)
                O("vector", lambda e, g=g, d=d: e.tensor_scalar_add(
                    out=x_res[d][:, g * HALF:(g + 1) * HALF],
                    in0=x_res[d][:, g * HALF:(g + 1) * HALF],
                    scalar1=bcol(f"hb2g{g}", d)), [(s_dve, 1)])
        phases = [(0, 0), (0, 1), (1, 0), (1, 1)]
        base_dve = s_dve.count

        def load_a(p, pe_gate=None):
            g, half = phases[p]
            if which == 0:
                return emit_wload(
                    lambda k, g=g, half=half: W1T_d[g][
                        128 * k:128 * (k + 1),
                        2048 * half:2048 * (half + 1)],
                    8, True, wta, pe_gate=pe_gate,
                    spill=lambda k, g=g, half=half: wsp_w1[g, half, k])
            W("sync", s_pe, pe_gate)
            for kt in range(8):
                O("sync", lambda e, kt=kt, g=g, half=half: e.dma_start(
                    out=wta[kt][:], in_=wsp_w1[g, half, kt]), [(s_wl, 16)])
            return (s_wl, s_wl.count)

        def load_b(p, pe_gate=None):
            g, half = phases[p]
            if which == 0:
                return emit_wload(
                    lambda k, g=g, half=half: W2T_d[g][
                        2048 * half + 128 * k:2048 * half + 128 * (k + 1), :],
                    16, False, wtb, pe_gate=pe_gate,
                    spill=lambda k, g=g, half=half: wsp_w2[g, half, k])
            W("sync", s_pe, pe_gate)
            for kt in range(16):
                O("sync", lambda e, kt=kt, g=g, half=half: e.dma_start(
                    out=wtb[kt][:], in_=wsp_w2[g, half, kt]), [(s_wl, 16)])
            return (s_wl, s_wl.count)

        pg0 = 0 if which == 0 else pg_first
        rdy = {(0, 'a'): load_a(0, pe_gate=pg0),
               (0, 'b'): load_b(0, pe_gate=pg0)}
        last_stt = base_dve
        for p, (g, half) in enumerate(phases):
            ra = rdy[(p, 'a')]
            rb = rdy[(p, 'b')]
            for tt in range(2):
                c0 = g * HALF + tt * FT
                W("tensor", s_dve, base_dve)
                W("tensor", ra[0], ra[1])
                for m1 in range(16):
                    pst = (PSA if m1 % 2 == 0 else PSB)[:, 0:FT]
                    W("tensor", s_act)
                    for k in range(8):
                        O("tensor", lambda e, m1=m1, k=k, c0=c0, pst=pst: e.matmul(
                            out=pst, lhsT=wta[k][:, m1 * 128:(m1 + 1) * 128],
                            rhs=x_ln[k][:, c0:c0 + FT],
                            start=(k == 0), stop=(k == 7)),
                            [(s_pe, 1)] if k == 7 else [])
                    W("scalar", s_pe)
                    O("scalar", lambda e, m1=m1, g=g, half=half, pst=pst: e.activation(
                        out=kt16[m1][:], in_=pst, func=AF.Identity,
                        scale=1.0, bias=bcol(f"b1g{g}", half * 16 + m1)),
                        [(s_act, 1)])
                if tt == 1 and p + 1 < 4:
                    rdy[(p + 1, 'a')] = load_a(p + 1, pe_gate=s_pe.count)
                W("tensor", rb[0], rb[1])
                for grp in range(2):
                    W("tensor", s_act)
                    W("tensor", s_dve, last_stt)
                    for k2 in range(16):
                        for mg in range(4):
                            m2 = grp * 4 + mg
                            O("tensor", lambda e, m2=m2, mg=mg, k2=k2: e.matmul(
                                out=PSW[:, mg * FT:(mg + 1) * FT],
                                lhsT=wtb[k2][:, m2 * 128:(m2 + 1) * 128],
                                rhs=kt16[k2][:],
                                start=(k2 == 0), stop=(k2 == 15)),
                                [(s_pe, 1)] if (k2 == 15 and mg == 3) else [])
                    W("vector", s_pe)
                    for mg in range(4):
                        m2 = grp * 4 + mg
                        O("vector", lambda e, m2=m2, mg=mg, c0=c0: e.scalar_tensor_tensor(
                            out=x_res[m2][:, c0:c0 + FT],
                            in0=PSW[:, mg * FT:(mg + 1) * FT], scalar=0.5,
                            in1=x_res[m2][:, c0:c0 + FT],
                            op0=ALU.mult, op1=ALU.add), [(s_dve, 1)])
                    last_stt = s_dve.count
            if p + 1 < 4:
                rdy[(p + 1, 'b')] = load_b(p + 1, pe_gate=s_pe.count)

    emit_ln()
    if DEBUG:
        W("sync", s_dve)
        for dd in range(8):
            O("sync", lambda e, dd=dd: e.dma_start(
                out=dbg_xln[dd], in_=x_ln[dd][:]), [(s_dbg, 16)])
    emit_ffn(0)
    if DEBUG:
        W("sync", s_dve)
        for dd in range(8):
            O("sync", lambda e, dd=dd: e.dma_start(
                out=dbg_xres[dd], in_=x_res[dd][:]), [(s_dbg, 16)])
    pe_pre_ln2 = s_pe.count
    emit_ln()

    emit_wload(lambda k: WiT_d[128 * k:128 * (k + 1), DI:2 * DI],
               8, True, wta, spill=lambda k: wsp_wiz[k], pe_gate=pe_pre_ln2)
    emit_wload(lambda k: WoT_d[128 * k:128 * (k + 1), :],
               16, False, wtb, spill=lambda k: wsp_wo[k], pe_gate=pe_pre_ln2)
    emit_wload(lambda k: WiT_d[128 * k:128 * (k + 1), 0:DI], 8, True, wta,
               pe_gate=pe_pre_ln2)

    stg4 = [wsl[0], wsl[1], wsl[2], wsl[3]]
    stg_guard = [wl_state["slot_free"].get(i) for i in range(4)]
    send_slot = {}
    CC_X = [0, 0]
    W("tensor", s_dve)
    for hh in range(2):
        for m in range(16):
            slot = stg4[m % 4]
            for tq in range(2):
                tt = hh * 2 + tq
                pst = PSA if tq % 2 == 0 else PSB
                W("tensor", s_act)
                if m >= 4:
                    psi, pv = send_slot[(m - 4, hh)]
                    W("tensor", s_st4[psi], pv)
                elif hh == 1 and m < 4:
                    psi, pv = send_slot[(12 + m, 0)]
                    W("tensor", s_st4[psi], pv)
                for k in range(8):
                    O("tensor", lambda e, m=m, k=k, tt=tt, pst=pst: e.matmul(
                        out=pst[:], lhsT=wta[k][:, m * 128:(m + 1) * 128],
                        rhs=x_ln[k][:, tt * TT:(tt + 1) * TT],
                        start=(k == 0), stop=(k == 7)),
                        [(s_pe, 1)] if k == 7 else [])
            si = m % 4
            for tq in range(2):
                pst = PSA if tq % 2 == 0 else PSB
                W("scalar", s_pe)
                W("scalar", s_dve)
                gg = stg_guard[si]
                if gg is not None:
                    W("scalar", gg[0], gg[1])
                O("scalar", lambda e, m=m, tq=tq, slot=slot, pst=pst: e.activation(
                    out=slot[:, tq * TT:(tq + 1) * TT], in_=pst[:],
                    func=AF.Identity, scale=1.0, bias=bcol("bix", m)), [(s_act, 1)])
            W("sync", s_act)
            for q in range(2):
                O("sync", lambda e, m=m, hh=hh, q=q, slot=slot: e.dma_start(
                    out=send1[hh][m // 2, 128 * (m % 2):128 * (m % 2) + 128,
                                  q * TT:(q + 1) * TT],
                    in_=slot[:, q * TT:(q + 1) * TT]), [(s_st4[si], 16)])
            if m == 0:
                O("sync", lambda e, hh=hh, slot=slot: e.dma_start(
                    out=ag_s[hh][:, :], in_=slot[0:DTR, :]), [(s_st4[si], 16)])
            send_slot[(m, hh)] = (si, s_st4[si].count)
        for si in range(4):
            W("gpsimd", s_st4[si])
        O("gpsimd", lambda e, hh=hh: e.collective_compute(
            "AllToAll", ALU.bypass, replica_groups=RG, ins=[send1[hh][:]],
            outs=[recv1[hh][:]]), [(s_cc, 1)])
        O("gpsimd", lambda e, hh=hh: e.collective_compute(
            "AllGather", ALU.bypass, replica_groups=RG, ins=[ag_s[hh][:]],
            outs=[ag_r[hh][:]]), [(s_cc, 1)])
        CC_X[hh] = s_cc.count

    W("gpsimd", s_dve, ANEG_DONE)
    for i in range(32):
        d, n = i // NST, i % NST
        W("gpsimd", s_gps)
        O("gpsimd", lambda e, d=d, n=n, i=i: e.tensor_scalar_mul(
            out=gdiag[i][:], in0=ident_b[:], scalar1=gvec[d][:, n:n + 1]),
            [(s_gps, 1)])
    GD_DONE = s_gps.count

    def tmap2(b, tl):
        return tl // HALF, tl % HALF

    NIT = B * 2 * NCH
    bdc = [(b, d, c) for b in range(B) for d in range(2) for c in range(NCH)]
    gslot_free = {}
    dtmm_done = {}
    conv_done = {}
    cast_done = {}
    sig_done = {}
    cu_done = {}
    v_done = {}
    scan_last = {}
    dt_done = {}
    ymm_done = {}
    yevac_done = {}
    st_done = {}

    cx_done = {}

    def mx_dma(it):
        b, d, c = bdc[it]
        p4 = it % 4
        t0 = c * TC
        src, off = tmap2(b, t0)
        W("sync", s_cc, CC_X[b])
        if it >= 4:
            W("sync", s_dve, conv_done[it - 4])
            W("sync", s_act, cast_done[it - 4])
        if c == 0:
            W("gpsimd", s_cc, CC_X[b])
            O("gpsimd", lambda e, p4=p4: e.memset(cxb[p4][:, 0:3], 0.0),
              [(s_gps, 1)])
        else:
            psrc, poff = tmap2(b, t0 - 3)
            O("sync", lambda e, p4=p4, b=b, psrc=psrc, poff=poff, d=d: e.dma_start(
                out=cxb[p4][:, 0:3],
                in_=recv1[b][psrc, 128 * d:128 * d + 128, poff:poff + 3]),
                [(s_cx[it % 2], 16)])
        O("sync", lambda e, p4=p4, b=b, src=src, off=off, d=d: e.dma_start(
            out=cxb[p4][:, 3:3 + TC],
            in_=recv1[b][src, 128 * d:128 * d + 128, off:off + TC]),
            [(s_cx[it % 2], 16)])
        O("sync", lambda e, b=b, src=src, off=off, p4=p4: e.dma_start(
            out=x32f[p4][:], in_=ag_r[b][src, :, off:off + TC]),
            [(s_cx[it % 2], 16)])
        cx_done[it] = s_cx[it % 2].count

    def mx_front(it):
        b, d, c = bdc[it]
        pp = it % 2
        p4 = it % 4
        W("scalar", s_cx[pp], cx_done[it])
        if it >= 2:
            W("scalar", s_pe, dtmm_done.get(it - 2))
        O("scalar", lambda e, pp=pp, p4=p4: e.activation(
            out=x32b[pp][:], in_=x32f[p4][:], func=AF.Copy, scale=1.0),
            [(s_act, 1)])
        cast_done[it] = s_act.count
        pdt = PSA if pp == 0 else PSB
        W("tensor", s_act, cast_done[it])
        O("tensor", lambda e, pp=pp, d=d, pdt=pdt: e.matmul(
            out=pdt[:, 0:TC], lhsT=wdtT[:, 128 * d:128 * d + 128],
            rhs=x32b[pp][:], start=True, stop=True), [(s_pe, 1)])
        dtmm_done[it] = s_pe.count
        W("scalar", s_pe, dtmm_done[it])
        O("scalar", lambda e, d=d, pdt=pdt: e.activation(
            out=ebuf[:], in_=pdt[:, 0:TC], func=AF.Exp, scale=1.0,
            bias=bcol("bdt", d)), [(s_act, 1)])
        if it >= 2:
            W("scalar", s_gps, v_done[it - 2])
        O("scalar", lambda e, pp=pp: e.activation(
            out=dtb[pp][:], in_=ebuf[:], func=AF.Ln, scale=1.0,
            bias=bcol("one")), [(s_act, 1)])
        dt_done[it] = s_act.count
        W("vector", s_cx[pp], cx_done[it])
        if it >= 2:
            W("vector", s_act, sig_done[it - 2])
        O("vector", lambda e, pp=pp, p4=p4, d=d: e.tensor_scalar_mul(
            out=cacc[pp][:], in0=cxb[p4][:, 3:3 + TC],
            scalar1=convw_s[d][:, 3:4]), [(s_dve, 1)])
        for k in range(3):
            O("vector", lambda e, pp=pp, p4=p4, d=d, k=k: e.scalar_tensor_tensor(
                out=cacc[pp][:], in0=cxb[p4][:, k:k + TC],
                scalar=convw_s[d][:, k:k + 1], in1=cacc[pp][:],
                op0=ALU.mult, op1=ALU.add), [(s_dve, 1)])
        conv_done[it] = s_dve.count

    def mx_gate(it):
        b, d, c = bdc[it]
        pp = it % 2
        W("scalar", s_dve, conv_done[it])
        if it >= 2:
            W("scalar", s_gps, v_done[it - 2])
        O("scalar", lambda e, pp=pp, d=d: e.activation(
            out=sigb[pp][:], in_=cacc[pp][:], func=AF.Silu,
            scale=1.0, bias=bcol("convb", d)), [(s_act, 1)])
        sig_done[it] = s_act.count
        W("gpsimd", s_act, sig_done[it])
        if it >= 2:
            W("gpsimd", s_dve, scan_last[it - 2])
        O("gpsimd", lambda e, pp=pp: e.tensor_mul(
            out=vbuf[pp][:], in0=sigb[pp][:], in1=dtb[pp][:]),
            [(s_gps, 1)])
        v_done[it] = s_gps.count

    def mx_back(it):
        b, d, c = bdc[it]
        pp = it % 2
        t0 = c * TC
        for n in range(NST):
            sl = n % NGS
            gf = gslot_free.get(sl)
            if gf is not None:
                W("scalar", s_dve, gf)
            W("scalar", s_act, dt_done[it])
            O("scalar", lambda e, n=n, pp=pp, d=d, sl=sl: e.activation(
                out=gbuf[:, sl * TC:(sl + 1) * TC], in_=dtb[pp][:],
                func=AF.Exp, scale=aneg[d][:, n:n + 1]), [(s_act, 1)])
            if n == 5 and it + 1 < NIT:
                mx_gate(it + 1)
            W("vector", s_act)
            W("vector", s_gps, v_done[it])
            if it >= 2:
                W("vector", s_pe, ymm_done[it - 2])
            init = 0.0 if c == 0 else \
                hbuf[1 - pp][:, n * TC + TC - 1:n * TC + TC]
            O("vector", lambda e, n=n, pp=pp, sl=sl, init=init:
              e.tensor_tensor_scan(
                out=hbuf[pp][:, n * TC:(n + 1) * TC],
                data0=gbuf[:, sl * TC:(sl + 1) * TC],
                data1=vbuf[pp][:], initial=init,
                op0=ALU.mult, op1=ALU.add), [(s_dve, 1)])
            gslot_free[sl] = s_dve.count
            W("tensor", s_dve)
            if n == 0 and it >= 2:
                W("tensor", s_act, yevac_done[it - 2])
            W("tensor", s_gps, GD_DONE)
            O("tensor", lambda e, n=n, pp=pp, d=d: e.matmul(
                out=PSY[pp][:], lhsT=gdiag[d * NST + n][:],
                rhs=hbuf[pp][:, n * TC:(n + 1) * TC],
                start=(n == 0), stop=(n == NST - 1)),
                [(s_pe, 1)] if n == NST - 1 else [])
        scan_last[it] = s_dve.count
        ymm_done[it] = s_pe.count
        W("scalar", s_pe, ymm_done[it])
        if it >= 2:
            W("scalar", s_yst, st_done[it - 2])
        O("scalar", lambda e, pp=pp: e.activation(
            out=ysb[pp][:], in_=PSY[pp][:], func=AF.Copy, scale=1.0),
            [(s_act, 1)])
        yevac_done[it] = s_act.count
        W("sync", s_act, yevac_done[it])
        dst, dcol = tmap2(b, t0)
        O("sync", lambda e, pp=pp, b=b, dst=dst, dcol=dcol, d=d: e.dma_start(
            out=ysend[b][dst, 128 * d:128 * d + 128, dcol:dcol + TC],
            in_=ysb[pp][:]), [(s_yst, 16)])
        st_done[it] = s_yst.count

    CC_Y = [0, 0]
    mx_dma(0)
    mx_dma(1)
    mx_front(0)
    mx_gate(0)
    for it in range(NIT):
        if it + 2 < NIT:
            mx_dma(it + 2)
        if it + 1 < NIT:
            mx_front(it + 1)
        mx_back(it)
        if it == NIT // 2 - 1:
            W("gpsimd", s_yst, st_done[it])
            O("gpsimd", lambda e: e.collective_compute(
                "AllToAll", ALU.bypass, replica_groups=RG, ins=[ysend[0][:]],
                outs=[yrecv[0][:]]), [(s_cc, 1)])
            CC_Y[0] = s_cc.count

    W("gpsimd", s_yst)
    O("gpsimd", lambda e: e.collective_compute(
        "AllToAll", ALU.bypass, replica_groups=RG, ins=[ysend[1][:]],
        outs=[yrecv[1][:]]), [(s_cc, 1)])
    CC_Y[1] = s_cc.count

    W("sync", s_pe)
    W("sync", s_yst)
    for kt in range(8):
        O("sync", lambda e, kt=kt: e.dma_start(out=wta[kt][:], in_=wsp_wiz[kt]),
          [(s_wl, 16)])
    for kt in range(16):
        O("sync", lambda e, kt=kt: e.dma_start(out=wtb[kt][:], in_=wsp_wo[kt]),
          [(s_wl, 16)])
    WL_P6 = s_wl.count
    for d in range(8):
        W("vector", s_dve)
        O("vector", lambda e, d=d: e.tensor_scalar_add(
            out=x_res[d][:], in0=x_res[d][:], scalar1=bcol("bo", d)), [(s_dve, 1)])
        W("vector", s_dve)
        O("vector", lambda e, d=d: e.scalar_tensor_tensor(
            out=x_res[d][:], in0=x_ln[d][:], scalar=bcol("dvm"),
            in1=x_res[d][:], op0=ALU.mult, op1=ALU.add), [(s_dve, 1)])
    W("tensor", s_dve)
    gate_done = {}
    for tt in range(4):
        for k in range(16):
            pp = k % 4
            pst = (PSA if k % 2 == 0 else PSB)[:, 0:FT]
            W("tensor", s_wl, WL_P6)
            W("tensor", s_act)
            for kk in range(8):
                O("tensor", lambda e, k=k, kk=kk, tt=tt, pst=pst: e.matmul(
                    out=pst, lhsT=wta[kk][:, k * 128:(k + 1) * 128],
                    rhs=x_ln[kk][:, tt * FT:(tt + 1) * FT],
                    start=(kk == 0), stop=(kk == 7)),
                    [(s_pe, 1)] if kk == 7 else [])
            W("sync", s_cc, CC_Y[tt // 2])
            idx = tt * 16 + k
            if idx >= 4:
                W("sync", s_dve, gate_done.get(idx - 4, 0))
            O("sync", lambda e, k=k, tt=tt, pp=pp: e.dma_start(
                out=yld[pp][:],
                in_=yrecv[tt // 2][k // 2, 128 * (k % 2):128 * (k % 2) + 128,
                                   (tt % 2) * FT:(tt % 2 + 1) * FT]),
                [(s_yld[pp], 16)])
            ycount = s_yld[pp].count
            W("scalar", s_pe)
            W("scalar", s_dve)
            O("scalar", lambda e, k=k, pst=pst: e.activation(
                out=szb[:], in_=pst, func=AF.Silu, scale=1.0,
                bias=bcol("biz", k)), [(s_act, 1)])
            W("vector", s_act)
            W("vector", s_yld[pp], ycount)
            W("vector", s_pe)
            O("vector", lambda e, k=k, pp=pp: e.tensor_mul(
                out=kt16[k][:], in0=yld[pp][:], in1=szb[:]), [(s_dve, 1)])
            gate_done[idx] = s_dve.count
        for grp in range(2):
            W("tensor", s_dve)
            for k2 in range(16):
                for mg in range(4):
                    m2 = grp * 4 + mg
                    O("tensor", lambda e, m2=m2, mg=mg, k2=k2: e.matmul(
                        out=PSW[:, mg * FT:(mg + 1) * FT],
                        lhsT=wtb[k2][:, m2 * 128:(m2 + 1) * 128],
                        rhs=kt16[k2][:], start=(k2 == 0), stop=(k2 == 15)),
                        [(s_pe, 1)] if (k2 == 15 and mg == 3) else [])
            W("vector", s_pe)
            for mg in range(4):
                m2 = grp * 4 + mg
                O("vector", lambda e, m2=m2, mg=mg, tt=tt: e.scalar_tensor_tensor(
                    out=x_res[m2][:, tt * FT:(tt + 1) * FT],
                    in0=PSW[:, mg * FT:(mg + 1) * FT], scalar=1.0,
                    in1=x_res[m2][:, tt * FT:(tt + 1) * FT],
                    op0=ALU.mult, op1=ALU.add), [(s_dve, 1)])

    pe_pre_ln3 = s_pe.count
    emit_ln()
    emit_ffn(1, pg_first=pe_pre_ln3)

    W("sync", s_dve, s_dve.count - 32)
    for d in range(8):
        O("sync", lambda e, d=d: e.dma_start(
            out=out_d[d][:, 0:HALF], in_=x_res[d][:, 0:HALF]), [(s_out, 16)])
    W("sync", s_dve)
    for d in range(8):
        O("sync", lambda e, d=d: e.dma_start(
            out=out_d[d][:, HALF:NT], in_=x_res[d][:, HALF:NT]), [(s_out, 16)])
    W("sync", s_out)
    if DEBUG:
        W("sync", s_dbg)

    from contextlib import ExitStack
    with ExitStack() as es:
        for s in p.sems.values():
            s.h = es.enter_context(nc.semaphore(s.name))
        block = es.enter_context(nc.Block())
        for ename in ENGS:
            ops = list(p.streams[ename])

            def emit_stream(e, ops=ops):
                for fn in ops:
                    fn(e)
            getattr(block, ename)(emit_stream)
    return nc


_NC_CACHE = None


def kernel(**inputs):
    global _NC_CACHE
    f32 = lambda a: np.ascontiguousarray(np.asarray(a, dtype=np.float32))
    x = f32(inputs["L0_attn_x"])
    Wi = f32(inputs["L0_attn_Wi"]); bi = f32(inputs["L0_attn_bi"])
    Wo = f32(inputs["L0_attn_Wo"]); bo = f32(inputs["L0_attn_bo"])
    Wdt = f32(inputs["L0_attn_Wdt"]); bdt = f32(inputs["L0_attn_bdt"])
    convw = f32(inputs["L0_attn_conv_w"])[:, 0, :]
    convb = f32(inputs["L0_attn_conv_b"])
    Alog = f32(inputs["L0_attn_A_log"]); Bm = f32(inputs["L0_attn_Bm"])
    Cm = f32(inputs["L0_attn_Cm"]); Dv = f32(inputs["L0_attn_Dv"])
    W1 = f32(inputs["L0_ffn_W1"]); b1 = f32(inputs["L0_ffn_b1"])
    W2 = f32(inputs["L0_ffn_W2"]); b2 = f32(inputs["L0_ffn_b2"])

    if _NC_CACHE is None:
        _NC_CACHE = build_nc()
    nc = _NC_CACHE

    WiT = np.ascontiguousarray(Wi.T)
    WoT = np.ascontiguousarray(Wo.T)
    W1T = {}
    W2T = {}
    for g in set([c // 2 for c in range(NCORES)] + [4 + c // 2 for c in range(NCORES)]):
        W1T[g] = np.ascontiguousarray(W1[g].T)
        W2T[g] = np.ascontiguousarray(W2[g].T)

    in_maps = []
    for c in range(NCORES):
        g0, g1 = c // 2, 4 + c // 2
        lo = 256 * c
        x_blk = np.concatenate([x[0, HALF * c:HALF * (c + 1)],
                                x[1, HALF * c:HALF * (c + 1)]], axis=0)
        biascat = np.concatenate([
            b1[g0].reshape(32, 128).T, b1[g1].reshape(32, 128).T,
            b2[g0].reshape(8, 128).T, b2[g1].reshape(8, 128).T,
            bi[:DI].reshape(16, 128).T, bi[DI:].reshape(16, 128).T,
            bo.reshape(8, 128).T,
            bdt[lo:lo + MYCH].reshape(2, 128).T,
            convb[lo:lo + MYCH].reshape(2, 128).T,
        ], axis=1)
        assert biascat.shape == (128, NB_DMA)
        in_maps.append(dict(
            xT_blk=np.ascontiguousarray(x_blk.T),
            WiT=WiT, WoT=WoT,
            W1Tg0=W1T[g0], W2Tg0=W2T[g0],
            W1Tg1=W1T[g1], W2Tg1=W2T[g1],
            WdtT_my=np.ascontiguousarray(Wdt[lo:lo + MYCH].T),
            biascat=np.ascontiguousarray(biascat),
            convw_my=np.ascontiguousarray(convw[lo:lo + MYCH]),
            Alog_my=np.ascontiguousarray(Alog[lo:lo + MYCH]),
            Bm_my=np.ascontiguousarray(Bm[lo:lo + MYCH]),
            Cm_my=np.ascontiguousarray(Cm[lo:lo + MYCH]),
            Dv=Dv,
        ))
    res = run_bass_kernel_spmd(nc, in_maps, core_ids=list(range(NCORES)))
    out = np.empty((B, T, DM), np.float32)
    for c in range(NCORES):
        o = res.results[c]["out"].reshape(DM, NT)
        out[0, HALF * c:HALF * (c + 1)] = o[:, :HALF].T
        out[1, HALF * c:HALF * (c + 1)] = o[:, HALF:].T
    return out


if __name__ == "__main__":
    import reference as refmod
    import jax
    with jax.default_device(jax.devices("cpu")[0]):
        inp = refmod.setup_inputs(0)
        ref = np.asarray(refmod.reference(**inp))
        inp_np = {k: np.asarray(v) for k, v in inp.items()}
    got = kernel(**inp_np)
    d = got - ref
    rvr = float((d ** 2).mean() / (ref.astype(np.float64) ** 2).mean())
    print("rvr:", rvr, "maxabs:", np.abs(d).max())


# version 28
# speedup vs baseline: 1.0186x; 1.0047x over previous
import os
os.environ.setdefault("TRN_ALLOW_ICI_ALL_TO_ALL", "1")

import numpy as np
import concourse.bass as bass
import concourse.mybir as mybir
from concourse.bass_utils import run_bass_kernel_spmd

F32 = mybir.dt.float32
BF16 = mybir.dt.bfloat16
AF = mybir.ActivationFunctionType
ALU = mybir.AluOpType
AX = mybir.AxisListType

NCORES = 8
B = 2
T = 8192
DM = 1024
DI = 2048
NST = 16
DTR = 32
NT = 2048
HALF = 1024
MYCH = 256
TC = 512
NCH = T // TC
TT = 512
TTL = 512
RG = [list(range(NCORES))]
KB = 1024

ENGS = ("sync", "gpsimd", "vector", "scalar", "tensor")
DEBUG = False

BV = {"dvm": 0, "b1g0": 1, "b1g1": 33, "b2g0": 65, "b2g1": 73,
      "bix": 81, "biz": 97, "bo": 113, "bdt": 121, "convb": 123,
      "hb2g0": 125, "hb2g1": 133, "eps": 141, "one": 142}
NB_DMA = 124


class Sem:
    __slots__ = ("name", "count", "h")

    def __init__(self, name):
        self.name, self.count, self.h = name, 0, None


class Prog:
    def __init__(self):
        self.streams = {e: [] for e in ENGS}
        self.sems = {}
        self.marks = {e: {} for e in ENGS}

    def sem(self, name):
        if name not in self.sems:
            self.sems[name] = Sem(name)
        return self.sems[name]

    def wait(self, eng, sem, val=None):
        v = sem.count if val is None else val
        if v <= 0 or self.marks[eng].get(sem.name, -1) >= v:
            return
        self.marks[eng][sem.name] = v
        self.streams[eng].append(lambda e, s=sem, v=v: e.wait_ge(s.h, v))

    def op(self, eng, fn, incs=()):
        for s, n in incs:
            s.count += n

        def clo(e, fn=fn, incs=tuple(incs)):
            inst = fn(e)
            for s, n in incs:
                inst.then_inc(s.h, n)
            return inst
        self.streams[eng].append(clo)


def build_nc():
    nc = bass.Bass("TRN2", target_bir_lowering=False)
    p = Prog()
    W = p.wait
    O = p.op

    def din(name, shape):
        return nc.dram_tensor(name, list(shape), F32, kind="ExternalInput").ap()

    x_d = din("xT_blk", (DM, NT))
    WiT_d = din("WiT", (DM, 2 * DI))
    WoT_d = din("WoT", (DI, DM))
    W1T_d = [din(f"W1Tg{i}", (DM, 4096)) for i in range(2)]
    W2T_d = [din(f"W2Tg{i}", (4096, DM)) for i in range(2)]
    WdtT_d = din("WdtT_my", (DTR, MYCH))
    biascat_d = din("biascat", (128, NB_DMA))
    convw_d = din("convw_my", (MYCH, 4))
    Alog_d = din("Alog_my", (MYCH, NST))
    Bm_d = din("Bm_my", (MYCH, NST))
    Cm_d = din("Cm_my", (MYCH, NST))
    Dv_d = din("Dv", (DI,))
    out_d = nc.dram_tensor("out", [8, 128, NT], F32, kind="ExternalOutput").ap()
    if DEBUG:
        dbg_xln = nc.dram_tensor("dbg_xln", [8, 128, NT], BF16,
                                 kind="ExternalOutput").ap()
        dbg_xres = nc.dram_tensor("dbg_xres", [8, 128, NT], F32,
                                  kind="ExternalOutput").ap()
        s_dbg = p.sem("s_dbg")

    send1 = [nc.dram_tensor(f"send1{b}", [NCORES, MYCH, HALF], F32,
                            kind="Internal").ap() for b in range(2)]
    recv1 = [nc.dram_tensor(f"recv1{b}", [NCORES, MYCH, HALF], F32,
                            kind="Internal").ap() for b in range(2)]
    ag_s = [nc.dram_tensor(f"ag_s{b}", [DTR, HALF], F32,
                           kind="Internal").ap() for b in range(2)]
    ag_r = [nc.dram_tensor(f"ag_r{b}", [NCORES, DTR, HALF], F32,
                           kind="Internal").ap() for b in range(2)]
    ysend = [nc.dram_tensor(f"ysend{b}", [NCORES, MYCH, HALF], F32,
                            kind="Internal").ap() for b in range(2)]
    yrecv = [nc.dram_tensor(f"yrecv{b}", [NCORES, MYCH, HALF], F32,
                            kind="Internal").ap() for b in range(2)]
    wsp_w1 = nc.dram_tensor("wsp_w1", [2, 2, 8, 128, 2048], BF16, kind="Internal").ap()
    wsp_w2 = nc.dram_tensor("wsp_w2", [2, 2, 16, 128, 1024], BF16, kind="Internal").ap()
    wsp_wiz = nc.dram_tensor("wsp_wiz", [8, 128, 2048], BF16, kind="Internal").ap()
    wsp_wo = nc.dram_tensor("wsp_wo", [16, 128, 1024], BF16, kind="Internal").ap()

    def S(name, shape, dt=F32):
        return nc.alloc_sbuf_tensor(name, list(shape), dt).ap()

    x_res = [S(f"xres{i}", (128, NT)) for i in range(8)]
    x_ln = [S(f"xln{i}", (128, NT), BF16) for i in range(8)]
    biasv = S("biasv", (128, 144))
    ident_f = S("ident_f", (128, 128))
    ident_b = S("ident_b", (128, 128), BF16)
    ones_colf = S("ones_colf", (128, 1))
    ones_colb = S("ones_colb", (128, 1), BF16)
    ones_rowf = S("ones_rowf", (1, 128))
    aneg = [S(f"aneg{d}", (128, NST)) for d in range(2)]
    gvec = [S(f"gvec{d}", (128, NST)) for d in range(2)]
    convw_s = [S(f"convw{d}", (128, 4)) for d in range(2)]
    cst16 = [S(f"cst16_{i}", (128, NST)) for i in range(2)]
    dv16 = S("dv16", (128, NST))
    wdtT = S("wdtT", (DTR, MYCH), BF16)
    wdtstg = S("wdtstg", (DTR, MYCH))

    POOL_KB = 96
    pool = nc.alloc_sbuf_tensor("pool", [128, POOL_KB * KB], mybir.dt.uint8)
    pool_base = nc.lookup_mloc(pool).addr

    def V(name, shape, dt, off_bytes):
        per_part = int(np.prod(shape[1:])) * mybir.dt.size(dt)
        assert off_bytes + per_part <= POOL_KB * KB, (name, off_bytes, per_part)
        h = nc.virtual_sbuf_tensor(name, list(shape), dt)
        nc._tensor(h.name, h.shape, h.dtype, type="SB")
        mloc = nc.lookup_mloc(h)
        mloc.allocated = True
        mloc.addr = pool_base + off_bytes
        return h.ap()

    FT = 512
    wta = [V(f"wta{i}", (128, 2048), BF16, (0 + 4 * i) * KB) for i in range(8)]
    wtb = [V(f"wtb{i}", (128, 1024), BF16, (32 + 2 * i) * KB) for i in range(16)]
    kt16 = [V(f"kt16_{i}", (128, FT), BF16, (64 + i) * KB) for i in range(16)]
    wsl = [V(f"wsl{i}", (128, 1024), F32, (80 + 4 * i) * KB) for i in range(4)]
    wslp = [V(f"wslp{i}", (128, 2048), F32, (80 + 8 * i) * KB) for i in range(2)]
    sqb = [V(f"sqb{i}", (128, TTL), BF16, (64 + i) * KB) for i in range(2)]
    xcb = [V(f"xcb{i}", (128, TTL), BF16, (66 + i) * KB) for i in range(2)]
    lntmp = V("lntmp", (128, TTL), F32, 68 * KB)
    lnA = V("lnA", (1, TTL), F32, 70 * KB)
    lnB = V("lnB", (1, TTL), F32, 72 * KB)
    lnC = V("lnC", (1, TTL), F32, 74 * KB)
    NGS = 4
    gbuf = V("gbuf", (128, NGS * TC), F32, 0 * KB)
    hbuf = [V(f"hbuf{i}", (128, NST * TC), BF16, (8 + 16 * i) * KB) for i in range(2)]
    mx = 40 * KB

    def MV(name, shape, dt):
        nonlocal mx
        per_part = int(np.prod(shape[1:])) * mybir.dt.size(dt)
        ap = V(name, shape, dt, mx)
        mx += (per_part + 31) // 32 * 32
        return ap

    ysb = [MV(f"ysb{i}", (128, TC), F32) for i in range(2)]
    ctmp = MV("ctmp", (128, TC), F32)
    cxb = [MV(f"cx{i}", (128, TC + 3), F32) for i in range(4)]
    dtb = [MV(f"dt{i}", (128, TC), F32) for i in range(2)]
    cacc = [MV(f"cacc{i}", (128, TC), F32) for i in range(2)]
    sigb = [MV(f"sig{i}", (128, TC), F32) for i in range(2)]
    vbuf = [MV(f"v{i}", (128, TC), F32) for i in range(2)]
    ebuf = MV("ebuf", (128, TC), F32)
    x32f = [MV(f"x32f{i}", (DTR, TC), F32) for i in range(4)]
    x32b = [MV(f"x32b{i}", (DTR, TC), BF16) for i in range(2)]
    gdiag = [MV(f"gdt{i}", (128, 128), BF16) for i in range(32)]
    assert mx <= 96 * KB, mx
    yld = [V(f"yld{i}", (128, FT), F32, (80 + 2 * i) * KB) for i in range(4)]
    szb = V("szb", (128, FT), F32, 88 * KB)

    PSA = nc.alloc_psum_tensor("psa", [128, 512], F32).ap()
    PSB = nc.alloc_psum_tensor("psb", [128, 512], F32).ap()
    PSW = nc.alloc_psum_tensor("psw", [128, 2048], F32).ap()
    PSYx = nc.alloc_psum_tensor("psy", [128, 2 * TC], F32).ap()
    PSY = [PSYx[:, 0:TC], PSYx[:, TC:2 * TC]]

    s_pe = p.sem("s_pe")
    s_act = p.sem("s_act")
    s_dve = p.sem("s_dve")
    s_gps = p.sem("s_gps")
    s_x = p.sem("s_x")
    s_w = p.sem("s_w")
    s_wr = p.sem("s_wr")
    s_sp = p.sem("s_sp")
    s_wl = p.sem("s_wl")
    s_cc = p.sem("s_cc")
    s_cx = [p.sem("s_cx0"), p.sem("s_cx1")]
    s_yst = p.sem("s_yst")
    s_yld = [p.sem(f"s_yld{i}") for i in range(4)]
    s_st4 = [p.sem(f"s_st4_{i}") for i in range(4)]
    s_out = p.sem("s_out")

    def bcol(tag, i=0):
        return biasv[:, BV[tag] + i:BV[tag] + i + 1]

    O("gpsimd", lambda e: e.memset(ident_f[:], 0.0), [(s_gps, 1)])
    W("gpsimd", s_gps)
    O("gpsimd", lambda e: e.affine_select(
        out=ident_f[:], in_=ident_f[:], compare_op=ALU.not_equal, fill=1.0,
        base=0, pattern=[[-1, 128]], channel_multiplier=1), [(s_gps, 1)])
    W("gpsimd", s_gps)
    O("gpsimd", lambda e: e.tensor_copy(out=ident_b[:], in_=ident_f[:]), [(s_gps, 1)])
    O("gpsimd", lambda e: e.memset(ones_colf[:], 1.0), [(s_gps, 1)])
    O("gpsimd", lambda e: e.memset(ones_colb[:], 1.0), [(s_gps, 1)])
    O("gpsimd", lambda e: e.memset(ones_rowf[:], 1.0), [(s_gps, 1)])
    O("gpsimd", lambda e: e.memset(biasv[:, BV["eps"]:BV["eps"] + 1], 1e-5),
      [(s_gps, 1)])
    O("gpsimd", lambda e: e.memset(biasv[:, BV["one"]:BV["one"] + 1], 1.0),
      [(s_gps, 1)])
    GPS_CONST = s_gps.count

    for d in range(8):
        O("sync", lambda e, d=d: e.dma_start(
            out=x_res[d][:], in_=x_d[d * 128:(d + 1) * 128, :]), [(s_x, 16)])

    def ld(dst, src, sem=s_w):
        O("sync", lambda e, dst=dst, src=src: e.dma_start(out=dst, in_=src),
          [(sem, 16)])

    ld(biasv[:, 1:1 + NB_DMA], biascat_d[:, :])
    ld(cst16[0][:], Bm_d[0:128, :])
    ld(cst16[1][:], Bm_d[128:256, :])
    ld(aneg[0][:], Cm_d[0:128, :])
    ld(aneg[1][:], Cm_d[128:256, :])
    ld(dv16[:], Dv_d.rearrange("(a b) -> a b", a=128))
    ld(convw_s[0][:], convw_d[0:128, :])
    ld(convw_s[1][:], convw_d[128:256, :])
    ld(wdtstg[:], WdtT_d[:, :])
    W("vector", s_w)
    for d in range(2):
        O("vector", lambda e, d=d: e.tensor_mul(
            out=gvec[d][:], in0=cst16[d][:], in1=aneg[d][:]), [(s_dve, 1)])
    GV_DONE = s_dve.count
    W("sync", s_dve, GV_DONE)
    ld(cst16[0][:], Alog_d[0:128, :])
    ld(cst16[1][:], Alog_d[128:256, :])
    W("scalar", s_w)
    for d in range(2):
        O("scalar", lambda e, d=d: e.activation(
            out=cst16[d][:], in_=cst16[d][:], func=AF.Exp, scale=1.0),
            [(s_act, 1)])
    O("scalar", lambda e: e.activation(
        out=wdtT[:], in_=wdtstg[:], func=AF.Copy, scale=1.0), [(s_act, 1)])
    W("vector", s_act)
    for d in range(2):
        O("vector", lambda e, d=d: e.tensor_scalar_mul(
            out=aneg[d][:], in0=cst16[d][:], scalar1=-1.0), [(s_dve, 1)])
    ANEG_DONE = s_dve.count
    for g in range(2):
        O("vector", lambda e, g=g: e.tensor_scalar_mul(
            out=biasv[:, BV[f"hb2g{g}"]:BV[f"hb2g{g}"] + 8],
            in0=biasv[:, BV[f"b2g{g}"]:BV[f"b2g{g}"] + 8], scalar1=0.5),
            [(s_dve, 1)])

    O("vector", lambda e: e.reduce_sum(
        out=dv16[:, 0:1], in_=dv16[:], axis=AX.X), [(s_dve, 1)])
    W("tensor", s_dve)
    W("tensor", s_gps, GPS_CONST)
    O("tensor", lambda e: e.transpose(
        out=PSA[0:1, 0:128], in_=dv16[:, 0:1],
        identity=ident_f[:]), [(s_pe, 1)])
    W("vector", s_pe)
    O("vector", lambda e: e.reduce_sum(
        out=wdtstg[0:1, 0:1], in_=PSA[0:1, 0:128], axis=AX.X), [(s_dve, 1)])
    W("tensor", s_dve)
    O("tensor", lambda e: e.matmul(
        out=PSB[0:128, 0:1], lhsT=ones_rowf[:], rhs=wdtstg[0:1, 0:1],
        start=True, stop=True), [(s_pe, 1)])
    W("scalar", s_pe)
    O("scalar", lambda e: e.activation(
        out=biasv[:, 0:1], in_=PSB[0:128, 0:1], func=AF.Copy, scale=1.0 / DI),
        [(s_act, 1)])

    ln_state = {"first": True}

    def emit_ln():
        grp_done = []
        for g in range(NT // TTL):
            c0 = g * TTL
            sq_pe = {}
            for d in range(8):
                pp = d % 2
                if ln_state["first"]:
                    W("scalar", s_x)
                    W("vector", s_x)
                W("scalar", s_dve)
                W("scalar", s_pe, sq_pe.get(d - 2))
                O("scalar", lambda e, d=d, pp=pp, c0=c0: e.activation(
                    out=sqb[pp][:], in_=x_res[d][:, c0:c0 + TTL],
                    func=AF.Square, scale=1.0), [(s_act, 1)])
                W("vector", s_pe, sq_pe.get(d - 2))
                O("vector", lambda e, d=d, pp=pp, c0=c0: e.tensor_copy(
                    out=xcb[pp][:], in_=x_res[d][:, c0:c0 + TTL]), [(s_dve, 1)])
                if ln_state["first"]:
                    W("tensor", s_gps, GPS_CONST)
                W("tensor", s_dve)
                W("tensor", s_act)
                O("tensor", lambda e, d=d, pp=pp: e.matmul(
                    out=PSA[0:1, 0:TTL], lhsT=ones_colb[:], rhs=xcb[pp][:],
                    start=(d == 0), stop=(d == 7)), [])
                O("tensor", lambda e, d=d, pp=pp: e.matmul(
                    out=PSB[0:1, 0:TTL], lhsT=ones_colb[:], rhs=sqb[pp][:],
                    start=(d == 0), stop=(d == 7)), [(s_pe, 1)])
                sq_pe[d] = s_pe.count
            ln_state["first"] = False
            W("vector", s_pe)
            O("vector", lambda e: e.tensor_scalar_mul(
                out=lnA[0:1, :], in0=PSA[0:1, 0:TTL], scalar1=1.0 / DM),
                [(s_dve, 1)])
            W("vector", s_dve)
            O("vector", lambda e: e.tensor_scalar_mul(
                out=lnB[0:1, :], in0=PSB[0:1, 0:TTL], scalar1=1.0 / DM),
                [(s_dve, 1)])
            W("vector", s_dve)
            O("vector", lambda e: e.tensor_mul(
                out=lnC[0:1, :], in0=lnA[0:1, :], in1=lnA[0:1, :]),
                [(s_dve, 1)])
            W("vector", s_dve)
            O("vector", lambda e: e.tensor_sub(
                out=lnB[0:1, :], in0=lnB[0:1, :], in1=lnC[0:1, :]),
                [(s_dve, 1)])
            W("scalar", s_dve)
            O("scalar", lambda e: e.activation(
                out=lnC[0:1, :], in_=lnB[0:1, :], func=AF.Sqrt,
                scale=1.0, bias=biasv[0:1, BV["eps"]:BV["eps"] + 1]),
                [(s_act, 1)])
            W("vector", s_act)
            O("vector", lambda e: e.reciprocal(
                out=lnB[0:1, :], in_=lnC[0:1, :]), [(s_dve, 1)])
            W("tensor", s_dve)
            O("tensor", lambda e: e.matmul(
                out=PSA[:, 0:TTL], lhsT=ones_rowf[:], rhs=lnA[0:1, :],
                start=True, stop=True), [(s_pe, 1)])
            O("tensor", lambda e: e.matmul(
                out=PSB[:, 0:TTL], lhsT=ones_rowf[:], rhs=lnB[0:1, :],
                start=True, stop=True), [(s_pe, 1)])
            W("vector", s_pe)
            for d in range(8):
                O("vector", lambda e, d=d, c0=c0: e.tensor_sub(
                    out=lntmp[:], in0=x_res[d][:, c0:c0 + TTL],
                    in1=PSA[:, 0:TTL]), [(s_dve, 1)])
                W("vector", s_dve)
                O("vector", lambda e, d=d, c0=c0: e.tensor_mul(
                    out=x_ln[d][:, c0:c0 + TTL], in0=lntmp[:],
                    in1=PSB[:, 0:TTL]), [(s_dve, 1)])
                W("vector", s_dve)
            grp_done.append(s_dve.count)
        return grp_done

    wl_state = {"slot_free": {}}

    def emit_wload(src_rows, ntiles, wide, dst, spill=None, pe_gate=None):
        eng = "vector"
        esem = s_dve
        nslot = 2 if wide else 4
        sp0 = s_sp.count
        pe0 = s_pe.count if pe_gate is None else pe_gate
        dma_done = {}
        cast_done = {}
        for k in range(ntiles):
            si = k % nslot
            slot = wslp[si] if wide else wsl[si]
            for sk in ([2 * si, 2 * si + 1] if wide else [si]):
                fr = wl_state["slot_free"].get(sk)
                if fr is not None:
                    W("sync", fr[0], fr[1])
            if k - nslot in cast_done:
                W("sync", esem, cast_done[k - nslot])
            O("sync", lambda e, k=k, slot=slot: e.dma_start(
                out=slot[:], in_=src_rows(k)), [(s_wr, 16)])
            dma_done[k] = s_wr.count
            W(eng, s_wr, dma_done[k])
            W(eng, s_pe, pe0)
            W(eng, s_sp, sp0)
            O(eng, lambda e, k=k, slot=slot: e.tensor_copy(
                out=dst[k][:], in_=slot[:]), [(esem, 1)])
            cast_done[k] = esem.count
            for sk in ([2 * si, 2 * si + 1] if wide else [si]):
                wl_state["slot_free"][sk] = (esem, cast_done[k])
            if spill is not None:
                W("sync", esem, cast_done[k])
                O("sync", lambda e, k=k: e.dma_start(
                    out=spill(k), in_=dst[k][:, :]), [(s_sp, 16)])
        return (esem, cast_done[ntiles - 1])

    def emit_ffn(which, pg_first=None):
        for g in range(2):
            for d in range(8):
                W("vector", s_pe)
                W("vector", s_act)
                W("vector", s_dve)
                O("vector", lambda e, g=g, d=d: e.tensor_scalar_add(
                    out=x_res[d][:, g * HALF:(g + 1) * HALF],
                    in0=x_res[d][:, g * HALF:(g + 1) * HALF],
                    scalar1=bcol(f"hb2g{g}", d)), [(s_dve, 1)])
        phases = [(0, 0), (0, 1), (1, 0), (1, 1)]
        base_dve = s_dve.count

        def load_a(p, pe_gate=None):
            g, half = phases[p]
            if which == 0:
                return emit_wload(
                    lambda k, g=g, half=half: W1T_d[g][
                        128 * k:128 * (k + 1),
                        2048 * half:2048 * (half + 1)],
                    8, True, wta, pe_gate=pe_gate,
                    spill=lambda k, g=g, half=half: wsp_w1[g, half, k])
            W("sync", s_pe, pe_gate)
            for kt in range(8):
                O("sync", lambda e, kt=kt, g=g, half=half: e.dma_start(
                    out=wta[kt][:], in_=wsp_w1[g, half, kt]), [(s_wl, 16)])
            return (s_wl, s_wl.count)

        def load_b(p, pe_gate=None):
            g, half = phases[p]
            if which == 0:
                return emit_wload(
                    lambda k, g=g, half=half: W2T_d[g][
                        2048 * half + 128 * k:2048 * half + 128 * (k + 1), :],
                    16, False, wtb, pe_gate=pe_gate,
                    spill=lambda k, g=g, half=half: wsp_w2[g, half, k])
            W("sync", s_pe, pe_gate)
            for kt in range(16):
                O("sync", lambda e, kt=kt, g=g, half=half: e.dma_start(
                    out=wtb[kt][:], in_=wsp_w2[g, half, kt]), [(s_wl, 16)])
            return (s_wl, s_wl.count)

        pg0 = 0 if which == 0 else pg_first
        rdy = {(0, 'a'): load_a(0, pe_gate=pg0),
               (0, 'b'): load_b(0, pe_gate=pg0)}
        last_stt = base_dve
        for p, (g, half) in enumerate(phases):
            ra = rdy[(p, 'a')]
            rb = rdy[(p, 'b')]
            for tt in range(2):
                c0 = g * HALF + tt * FT
                W("tensor", s_dve, base_dve)
                W("tensor", ra[0], ra[1])
                for m1 in range(16):
                    pst = (PSA if m1 % 2 == 0 else PSB)[:, 0:FT]
                    W("tensor", s_act)
                    for k in range(8):
                        O("tensor", lambda e, m1=m1, k=k, c0=c0, pst=pst: e.matmul(
                            out=pst, lhsT=wta[k][:, m1 * 128:(m1 + 1) * 128],
                            rhs=x_ln[k][:, c0:c0 + FT],
                            start=(k == 0), stop=(k == 7)),
                            [(s_pe, 1)] if k == 7 else [])
                    W("scalar", s_pe)
                    O("scalar", lambda e, m1=m1, g=g, half=half, pst=pst: e.activation(
                        out=kt16[m1][:], in_=pst, func=AF.Identity,
                        scale=1.0, bias=bcol(f"b1g{g}", half * 16 + m1)),
                        [(s_act, 1)])
                if tt == 1 and p + 1 < 4:
                    rdy[(p + 1, 'a')] = load_a(p + 1, pe_gate=s_pe.count)
                W("tensor", rb[0], rb[1])
                for grp in range(2):
                    W("tensor", s_act)
                    W("tensor", s_dve, last_stt)
                    for k2 in range(16):
                        for mg in range(4):
                            m2 = grp * 4 + mg
                            O("tensor", lambda e, m2=m2, mg=mg, k2=k2: e.matmul(
                                out=PSW[:, mg * FT:(mg + 1) * FT],
                                lhsT=wtb[k2][:, m2 * 128:(m2 + 1) * 128],
                                rhs=kt16[k2][:],
                                start=(k2 == 0), stop=(k2 == 15)),
                                [(s_pe, 1)] if (k2 == 15 and mg == 3) else [])
                    W("vector", s_pe)
                    for mg in range(4):
                        m2 = grp * 4 + mg
                        O("vector", lambda e, m2=m2, mg=mg, c0=c0: e.scalar_tensor_tensor(
                            out=x_res[m2][:, c0:c0 + FT],
                            in0=PSW[:, mg * FT:(mg + 1) * FT], scalar=0.5,
                            in1=x_res[m2][:, c0:c0 + FT],
                            op0=ALU.mult, op1=ALU.add), [(s_dve, 1)])
                    last_stt = s_dve.count
            if p + 1 < 4:
                rdy[(p + 1, 'b')] = load_b(p + 1, pe_gate=s_pe.count)

    emit_ln()
    if DEBUG:
        W("sync", s_dve)
        for dd in range(8):
            O("sync", lambda e, dd=dd: e.dma_start(
                out=dbg_xln[dd], in_=x_ln[dd][:]), [(s_dbg, 16)])
    emit_ffn(0)
    if DEBUG:
        W("sync", s_dve)
        for dd in range(8):
            O("sync", lambda e, dd=dd: e.dma_start(
                out=dbg_xres[dd], in_=x_res[dd][:]), [(s_dbg, 16)])
    pe_pre_ln2 = s_pe.count
    emit_ln()

    emit_wload(lambda k: WiT_d[128 * k:128 * (k + 1), DI:2 * DI],
               8, True, wta, spill=lambda k: wsp_wiz[k], pe_gate=pe_pre_ln2)
    emit_wload(lambda k: WoT_d[128 * k:128 * (k + 1), :],
               16, False, wtb, spill=lambda k: wsp_wo[k], pe_gate=pe_pre_ln2)
    emit_wload(lambda k: WiT_d[128 * k:128 * (k + 1), 0:DI], 8, True, wta,
               pe_gate=pe_pre_ln2)

    stg4 = [wsl[0], wsl[1], wsl[2], wsl[3]]
    stg_guard = [wl_state["slot_free"].get(i) for i in range(4)]
    send_slot = {}
    CC_X = [0, 0]
    W("tensor", s_dve)
    for hh in range(2):
        for m in range(16):
            slot = stg4[m % 4]
            for tq in range(2):
                tt = hh * 2 + tq
                pst = PSA if tq % 2 == 0 else PSB
                W("tensor", s_act)
                if m >= 4:
                    psi, pv = send_slot[(m - 4, hh)]
                    W("tensor", s_st4[psi], pv)
                elif hh == 1 and m < 4:
                    psi, pv = send_slot[(12 + m, 0)]
                    W("tensor", s_st4[psi], pv)
                for k in range(8):
                    O("tensor", lambda e, m=m, k=k, tt=tt, pst=pst: e.matmul(
                        out=pst[:], lhsT=wta[k][:, m * 128:(m + 1) * 128],
                        rhs=x_ln[k][:, tt * TT:(tt + 1) * TT],
                        start=(k == 0), stop=(k == 7)),
                        [(s_pe, 1)] if k == 7 else [])
            si = m % 4
            for tq in range(2):
                pst = PSA if tq % 2 == 0 else PSB
                W("scalar", s_pe)
                W("scalar", s_dve)
                gg = stg_guard[si]
                if gg is not None:
                    W("scalar", gg[0], gg[1])
                O("scalar", lambda e, m=m, tq=tq, slot=slot, pst=pst: e.activation(
                    out=slot[:, tq * TT:(tq + 1) * TT], in_=pst[:],
                    func=AF.Identity, scale=1.0, bias=bcol("bix", m)), [(s_act, 1)])
            W("sync", s_act)
            for q in range(2):
                O("sync", lambda e, m=m, hh=hh, q=q, slot=slot: e.dma_start(
                    out=send1[hh][m // 2, 128 * (m % 2):128 * (m % 2) + 128,
                                  q * TT:(q + 1) * TT],
                    in_=slot[:, q * TT:(q + 1) * TT]), [(s_st4[si], 16)])
            if m == 0:
                O("sync", lambda e, hh=hh, slot=slot: e.dma_start(
                    out=ag_s[hh][:, :], in_=slot[0:DTR, :]), [(s_st4[si], 16)])
            send_slot[(m, hh)] = (si, s_st4[si].count)
        for si in range(4):
            W("gpsimd", s_st4[si])
        O("gpsimd", lambda e, hh=hh: e.collective_compute(
            "AllToAll", ALU.bypass, replica_groups=RG, ins=[send1[hh][:]],
            outs=[recv1[hh][:]]), [(s_cc, 1)])
        O("gpsimd", lambda e, hh=hh: e.collective_compute(
            "AllGather", ALU.bypass, replica_groups=RG, ins=[ag_s[hh][:]],
            outs=[ag_r[hh][:]]), [(s_cc, 1)])
        CC_X[hh] = s_cc.count

    W("gpsimd", s_dve, ANEG_DONE)
    for i in range(32):
        d, n = i // NST, i % NST
        W("gpsimd", s_gps)
        O("gpsimd", lambda e, d=d, n=n, i=i: e.tensor_scalar_mul(
            out=gdiag[i][:], in0=ident_b[:], scalar1=gvec[d][:, n:n + 1]),
            [(s_gps, 1)])
    GD_DONE = s_gps.count

    def tmap2(b, tl):
        return tl // HALF, tl % HALF

    NIT = B * 2 * NCH
    bdc = [(b, d, c) for b in range(B) for d in range(2) for c in range(NCH)]
    gslot_free = {}
    dtmm_done = {}
    conv_done = {}
    cast_done = {}
    sig_done = {}
    cu_done = {}
    v_done = {}
    scan_last = {}
    dt_done = {}
    ymm_done = {}
    yevac_done = {}
    st_done = {}

    cx_done = {}

    def mx_dma(it):
        b, d, c = bdc[it]
        p4 = it % 4
        t0 = c * TC
        src, off = tmap2(b, t0)
        W("sync", s_cc, CC_X[b])
        if it >= 4:
            W("sync", s_dve, conv_done[it - 4])
            W("sync", s_act, cast_done[it - 4])
        if c == 0:
            W("gpsimd", s_cc, CC_X[b])
            O("gpsimd", lambda e, p4=p4: e.memset(cxb[p4][:, 0:3], 0.0),
              [(s_gps, 1)])
        else:
            psrc, poff = tmap2(b, t0 - 3)
            O("sync", lambda e, p4=p4, b=b, psrc=psrc, poff=poff, d=d: e.dma_start(
                out=cxb[p4][:, 0:3],
                in_=recv1[b][psrc, 128 * d:128 * d + 128, poff:poff + 3]),
                [(s_cx[it % 2], 16)])
        O("sync", lambda e, p4=p4, b=b, src=src, off=off, d=d: e.dma_start(
            out=cxb[p4][:, 3:3 + TC],
            in_=recv1[b][src, 128 * d:128 * d + 128, off:off + TC]),
            [(s_cx[it % 2], 16)])
        O("sync", lambda e, b=b, src=src, off=off, p4=p4: e.dma_start(
            out=x32f[p4][:], in_=ag_r[b][src, :, off:off + TC]),
            [(s_cx[it % 2], 16)])
        cx_done[it] = s_cx[it % 2].count

    def mx_front(it):
        b, d, c = bdc[it]
        pp = it % 2
        p4 = it % 4
        W("scalar", s_cx[pp], cx_done[it])
        if it >= 2:
            W("scalar", s_pe, dtmm_done.get(it - 2))
        O("scalar", lambda e, pp=pp, p4=p4: e.activation(
            out=x32b[pp][:], in_=x32f[p4][:], func=AF.Copy, scale=1.0),
            [(s_act, 1)])
        cast_done[it] = s_act.count
        pdt = PSA if pp == 0 else PSB
        W("tensor", s_act, cast_done[it])
        O("tensor", lambda e, pp=pp, d=d, pdt=pdt: e.matmul(
            out=pdt[:, 0:TC], lhsT=wdtT[:, 128 * d:128 * d + 128],
            rhs=x32b[pp][:], start=True, stop=True), [(s_pe, 1)])
        dtmm_done[it] = s_pe.count
        W("scalar", s_pe, dtmm_done[it])
        O("scalar", lambda e, d=d, pdt=pdt: e.activation(
            out=ebuf[:], in_=pdt[:, 0:TC], func=AF.Exp, scale=1.0,
            bias=bcol("bdt", d)), [(s_act, 1)])
        if it >= 2:
            W("scalar", s_gps, v_done[it - 2])
        O("scalar", lambda e, pp=pp: e.activation(
            out=dtb[pp][:], in_=ebuf[:], func=AF.Ln, scale=1.0,
            bias=bcol("one")), [(s_act, 1)])
        dt_done[it] = s_act.count
        W("vector", s_cx[pp], cx_done[it])
        if it >= 2:
            W("vector", s_act, sig_done[it - 2])
        O("vector", lambda e, pp=pp, p4=p4, d=d: e.tensor_scalar_mul(
            out=cacc[pp][:], in0=cxb[p4][:, 3:3 + TC],
            scalar1=convw_s[d][:, 3:4]), [(s_dve, 1)])
        for k in range(3):
            O("vector", lambda e, pp=pp, p4=p4, d=d, k=k: e.scalar_tensor_tensor(
                out=cacc[pp][:], in0=cxb[p4][:, k:k + TC],
                scalar=convw_s[d][:, k:k + 1], in1=cacc[pp][:],
                op0=ALU.mult, op1=ALU.add), [(s_dve, 1)])
        conv_done[it] = s_dve.count

    def mx_gate(it):
        b, d, c = bdc[it]
        pp = it % 2
        W("scalar", s_dve, conv_done[it])
        if it >= 2:
            W("scalar", s_gps, v_done[it - 2])
        O("scalar", lambda e, pp=pp, d=d: e.activation(
            out=sigb[pp][:], in_=cacc[pp][:], func=AF.Silu,
            scale=1.0, bias=bcol("convb", d)), [(s_act, 1)])
        sig_done[it] = s_act.count
        W("gpsimd", s_act, sig_done[it])
        if it >= 2:
            W("gpsimd", s_dve, scan_last[it - 2])
        O("gpsimd", lambda e, pp=pp: e.tensor_mul(
            out=vbuf[pp][:], in0=sigb[pp][:], in1=dtb[pp][:]),
            [(s_gps, 1)])
        v_done[it] = s_gps.count

    def mx_back(it):
        b, d, c = bdc[it]
        pp = it % 2
        t0 = c * TC
        for n in range(NST):
            sl = n % NGS
            gf = gslot_free.get(sl)
            if gf is not None:
                W("scalar", s_dve, gf)
            W("scalar", s_act, dt_done[it])
            O("scalar", lambda e, n=n, pp=pp, d=d, sl=sl: e.activation(
                out=gbuf[:, sl * TC:(sl + 1) * TC], in_=dtb[pp][:],
                func=AF.Exp, scale=aneg[d][:, n:n + 1]), [(s_act, 1)])
            ge = s_act.count
            if n == 5 and it + 1 < NIT:
                mx_gate(it + 1)
            W("vector", s_act, ge)
            W("vector", s_gps, v_done[it])
            if it >= 2:
                W("vector", s_pe, ymm_done[it - 2])
            init = 0.0 if c == 0 else \
                hbuf[1 - pp][:, n * TC + TC - 1:n * TC + TC]
            O("vector", lambda e, n=n, pp=pp, sl=sl, init=init:
              e.tensor_tensor_scan(
                out=hbuf[pp][:, n * TC:(n + 1) * TC],
                data0=gbuf[:, sl * TC:(sl + 1) * TC],
                data1=vbuf[pp][:], initial=init,
                op0=ALU.mult, op1=ALU.add), [(s_dve, 1)])
            gslot_free[sl] = s_dve.count
            W("tensor", s_dve)
            if n == 0 and it >= 2:
                W("tensor", s_act, yevac_done[it - 2])
            W("tensor", s_gps, GD_DONE)
            O("tensor", lambda e, n=n, pp=pp, d=d: e.matmul(
                out=PSY[pp][:], lhsT=gdiag[d * NST + n][:],
                rhs=hbuf[pp][:, n * TC:(n + 1) * TC],
                start=(n == 0), stop=(n == NST - 1)),
                [(s_pe, 1)] if n == NST - 1 else [])
        scan_last[it] = s_dve.count
        ymm_done[it] = s_pe.count
        W("scalar", s_pe, ymm_done[it])
        if it >= 2:
            W("scalar", s_yst, st_done[it - 2])
        O("scalar", lambda e, pp=pp: e.activation(
            out=ysb[pp][:], in_=PSY[pp][:], func=AF.Copy, scale=1.0),
            [(s_act, 1)])
        yevac_done[it] = s_act.count
        W("sync", s_act, yevac_done[it])
        dst, dcol = tmap2(b, t0)
        O("sync", lambda e, pp=pp, b=b, dst=dst, dcol=dcol, d=d: e.dma_start(
            out=ysend[b][dst, 128 * d:128 * d + 128, dcol:dcol + TC],
            in_=ysb[pp][:]), [(s_yst, 16)])
        st_done[it] = s_yst.count

    CC_Y = [0, 0]
    mx_dma(0)
    mx_dma(1)
    mx_front(0)
    mx_gate(0)
    for it in range(NIT):
        if it + 2 < NIT:
            mx_dma(it + 2)
        if it + 1 < NIT:
            mx_front(it + 1)
        mx_back(it)
        if it == NIT // 2 - 1:
            W("gpsimd", s_yst, st_done[it])
            O("gpsimd", lambda e: e.collective_compute(
                "AllToAll", ALU.bypass, replica_groups=RG, ins=[ysend[0][:]],
                outs=[yrecv[0][:]]), [(s_cc, 1)])
            CC_Y[0] = s_cc.count

    W("gpsimd", s_yst)
    O("gpsimd", lambda e: e.collective_compute(
        "AllToAll", ALU.bypass, replica_groups=RG, ins=[ysend[1][:]],
        outs=[yrecv[1][:]]), [(s_cc, 1)])
    CC_Y[1] = s_cc.count

    W("sync", s_pe)
    W("sync", s_yst)
    for kt in range(8):
        O("sync", lambda e, kt=kt: e.dma_start(out=wta[kt][:], in_=wsp_wiz[kt]),
          [(s_wl, 16)])
    for kt in range(16):
        O("sync", lambda e, kt=kt: e.dma_start(out=wtb[kt][:], in_=wsp_wo[kt]),
          [(s_wl, 16)])
    WL_P6 = s_wl.count
    for d in range(8):
        W("vector", s_dve)
        O("vector", lambda e, d=d: e.tensor_scalar_add(
            out=x_res[d][:], in0=x_res[d][:], scalar1=bcol("bo", d)), [(s_dve, 1)])
        W("vector", s_dve)
        O("vector", lambda e, d=d: e.scalar_tensor_tensor(
            out=x_res[d][:], in0=x_ln[d][:], scalar=bcol("dvm"),
            in1=x_res[d][:], op0=ALU.mult, op1=ALU.add), [(s_dve, 1)])
    W("tensor", s_dve)
    gate_done = {}
    for tt in range(4):
        for k in range(16):
            pp = k % 4
            pst = (PSA if k % 2 == 0 else PSB)[:, 0:FT]
            W("tensor", s_wl, WL_P6)
            W("tensor", s_act)
            for kk in range(8):
                O("tensor", lambda e, k=k, kk=kk, tt=tt, pst=pst: e.matmul(
                    out=pst, lhsT=wta[kk][:, k * 128:(k + 1) * 128],
                    rhs=x_ln[kk][:, tt * FT:(tt + 1) * FT],
                    start=(kk == 0), stop=(kk == 7)),
                    [(s_pe, 1)] if kk == 7 else [])
            W("sync", s_cc, CC_Y[tt // 2])
            idx = tt * 16 + k
            if idx >= 4:
                W("sync", s_dve, gate_done.get(idx - 4, 0))
            O("sync", lambda e, k=k, tt=tt, pp=pp: e.dma_start(
                out=yld[pp][:],
                in_=yrecv[tt // 2][k // 2, 128 * (k % 2):128 * (k % 2) + 128,
                                   (tt % 2) * FT:(tt % 2 + 1) * FT]),
                [(s_yld[pp], 16)])
            ycount = s_yld[pp].count
            W("scalar", s_pe)
            W("scalar", s_dve)
            O("scalar", lambda e, k=k, pst=pst: e.activation(
                out=szb[:], in_=pst, func=AF.Silu, scale=1.0,
                bias=bcol("biz", k)), [(s_act, 1)])
            W("vector", s_act)
            W("vector", s_yld[pp], ycount)
            W("vector", s_pe)
            O("vector", lambda e, k=k, pp=pp: e.tensor_mul(
                out=kt16[k][:], in0=yld[pp][:], in1=szb[:]), [(s_dve, 1)])
            gate_done[idx] = s_dve.count
        for grp in range(2):
            W("tensor", s_dve)
            for k2 in range(16):
                for mg in range(4):
                    m2 = grp * 4 + mg
                    O("tensor", lambda e, m2=m2, mg=mg, k2=k2: e.matmul(
                        out=PSW[:, mg * FT:(mg + 1) * FT],
                        lhsT=wtb[k2][:, m2 * 128:(m2 + 1) * 128],
                        rhs=kt16[k2][:], start=(k2 == 0), stop=(k2 == 15)),
                        [(s_pe, 1)] if (k2 == 15 and mg == 3) else [])
            W("vector", s_pe)
            for mg in range(4):
                m2 = grp * 4 + mg
                O("vector", lambda e, m2=m2, mg=mg, tt=tt: e.scalar_tensor_tensor(
                    out=x_res[m2][:, tt * FT:(tt + 1) * FT],
                    in0=PSW[:, mg * FT:(mg + 1) * FT], scalar=1.0,
                    in1=x_res[m2][:, tt * FT:(tt + 1) * FT],
                    op0=ALU.mult, op1=ALU.add), [(s_dve, 1)])

    pe_pre_ln3 = s_pe.count
    emit_ln()
    emit_ffn(1, pg_first=pe_pre_ln3)

    W("sync", s_dve, s_dve.count - 32)
    for d in range(8):
        O("sync", lambda e, d=d: e.dma_start(
            out=out_d[d][:, 0:HALF], in_=x_res[d][:, 0:HALF]), [(s_out, 16)])
    W("sync", s_dve)
    for d in range(8):
        O("sync", lambda e, d=d: e.dma_start(
            out=out_d[d][:, HALF:NT], in_=x_res[d][:, HALF:NT]), [(s_out, 16)])
    W("sync", s_out)
    if DEBUG:
        W("sync", s_dbg)

    from contextlib import ExitStack
    with ExitStack() as es:
        for s in p.sems.values():
            s.h = es.enter_context(nc.semaphore(s.name))
        block = es.enter_context(nc.Block())
        for ename in ENGS:
            ops = list(p.streams[ename])

            def emit_stream(e, ops=ops):
                for fn in ops:
                    fn(e)
            getattr(block, ename)(emit_stream)
    return nc


_NC_CACHE = None


def kernel(**inputs):
    global _NC_CACHE
    f32 = lambda a: np.ascontiguousarray(np.asarray(a, dtype=np.float32))
    x = f32(inputs["L0_attn_x"])
    Wi = f32(inputs["L0_attn_Wi"]); bi = f32(inputs["L0_attn_bi"])
    Wo = f32(inputs["L0_attn_Wo"]); bo = f32(inputs["L0_attn_bo"])
    Wdt = f32(inputs["L0_attn_Wdt"]); bdt = f32(inputs["L0_attn_bdt"])
    convw = f32(inputs["L0_attn_conv_w"])[:, 0, :]
    convb = f32(inputs["L0_attn_conv_b"])
    Alog = f32(inputs["L0_attn_A_log"]); Bm = f32(inputs["L0_attn_Bm"])
    Cm = f32(inputs["L0_attn_Cm"]); Dv = f32(inputs["L0_attn_Dv"])
    W1 = f32(inputs["L0_ffn_W1"]); b1 = f32(inputs["L0_ffn_b1"])
    W2 = f32(inputs["L0_ffn_W2"]); b2 = f32(inputs["L0_ffn_b2"])

    if _NC_CACHE is None:
        _NC_CACHE = build_nc()
    nc = _NC_CACHE

    WiT = np.ascontiguousarray(Wi.T)
    WoT = np.ascontiguousarray(Wo.T)
    W1T = {}
    W2T = {}
    for g in set([c // 2 for c in range(NCORES)] + [4 + c // 2 for c in range(NCORES)]):
        W1T[g] = np.ascontiguousarray(W1[g].T)
        W2T[g] = np.ascontiguousarray(W2[g].T)

    in_maps = []
    for c in range(NCORES):
        g0, g1 = c // 2, 4 + c // 2
        lo = 256 * c
        x_blk = np.concatenate([x[0, HALF * c:HALF * (c + 1)],
                                x[1, HALF * c:HALF * (c + 1)]], axis=0)
        biascat = np.concatenate([
            b1[g0].reshape(32, 128).T, b1[g1].reshape(32, 128).T,
            b2[g0].reshape(8, 128).T, b2[g1].reshape(8, 128).T,
            bi[:DI].reshape(16, 128).T, bi[DI:].reshape(16, 128).T,
            bo.reshape(8, 128).T,
            bdt[lo:lo + MYCH].reshape(2, 128).T,
            convb[lo:lo + MYCH].reshape(2, 128).T,
        ], axis=1)
        assert biascat.shape == (128, NB_DMA)
        in_maps.append(dict(
            xT_blk=np.ascontiguousarray(x_blk.T),
            WiT=WiT, WoT=WoT,
            W1Tg0=W1T[g0], W2Tg0=W2T[g0],
            W1Tg1=W1T[g1], W2Tg1=W2T[g1],
            WdtT_my=np.ascontiguousarray(Wdt[lo:lo + MYCH].T),
            biascat=np.ascontiguousarray(biascat),
            convw_my=np.ascontiguousarray(convw[lo:lo + MYCH]),
            Alog_my=np.ascontiguousarray(Alog[lo:lo + MYCH]),
            Bm_my=np.ascontiguousarray(Bm[lo:lo + MYCH]),
            Cm_my=np.ascontiguousarray(Cm[lo:lo + MYCH]),
            Dv=Dv,
        ))
    res = run_bass_kernel_spmd(nc, in_maps, core_ids=list(range(NCORES)))
    out = np.empty((B, T, DM), np.float32)
    for c in range(NCORES):
        o = res.results[c]["out"].reshape(DM, NT)
        out[0, HALF * c:HALF * (c + 1)] = o[:, :HALF].T
        out[1, HALF * c:HALF * (c + 1)] = o[:, HALF:].T
    return out


if __name__ == "__main__":
    import reference as refmod
    import jax
    with jax.default_device(jax.devices("cpu")[0]):
        inp = refmod.setup_inputs(0)
        ref = np.asarray(refmod.reference(**inp))
        inp_np = {k: np.asarray(v) for k, v in inp.items()}
    got = kernel(**inp_np)
    d = got - ref
    rvr = float((d ** 2).mean() / (ref.astype(np.float64) ** 2).mean())
    print("rvr:", rvr, "maxabs:", np.abs(d).max())
